# version 15
# baseline (speedup 1.0000x reference)
import numpy as np
import concourse.bass as bass
import concourse.mybir as mybir
from concourse.bass_utils import run_bass_kernel_spmd

F32 = mybir.dt.float32
BF16 = mybir.dt.bfloat16
ALU = mybir.AluOpType
AFT = mybir.ActivationFunctionType
AXL = mybir.AxisListType

B, L, D = 4, 8192, 1024
H2, DK, CH = 2, 256, 32
MAC, NSUB = 256, 2
NMAC = L // MAC
HALF = L // 2
MAC2, NSUB2 = 512, 4
NOM = HALF // MAC2
EPS_LN, EPS_L2, EPS_RMS = 1e-5, 1e-6, 1e-5
DBG_NMAC = None
DBG_XNT = True
DBG_DUMP = False
DBG_P2 = None
DBG_LEVEL = 9
DBG_PHASE2 = True
REC127 = 1.0 / 127.0

def CP(e, d, s):
    return e.tensor_copy(d, s) if hasattr(e, "tensor_copy") else e.copy(d, s)


ESEM = {"sync": "dX", "tensor": "T", "vector": "V", "scalar": "C",
        "gpsimd": "G"}


class Sched:
    def __init__(self):
        self.items = []
        self.buf = None
        self.tag = "pre"
        self.meta = []
        self.cnt = {"T": 0, "V": 0, "C": 0, "G": 0}
        for q in ("dX", "dG"):
            for i in range(4):
                self.cnt[q + str(i)] = 0
        self.ring = {"dX": 0, "dG": 0}
        self.wr = {}
        self.rd = {}
        self.seen = {}

    def _edge_waits(self, reads, writes):
        w = {}
        for k in reads:
            if k in self.wr:
                s, c = self.wr[k]
                if w.get(s, 0) < c:
                    w[s] = c
        for k in writes:
            if k in self.wr:
                s, c = self.wr[k]
                if w.get(s, 0) < c:
                    w[s] = c
            for s, c in self.rd.get(k, {}).items():
                if w.get(s, 0) < c:
                    w[s] = c
        return w

    def begin(self):
        self.buf = []

    def take(self):
        b = self.buf
        self.buf = None
        return b

    def emit(self, lst):
        for a in lst:
            if len(a) > 7:
                self.tag = a[7]
                self.mmw = a[8]
            self.op(*a[:7])

    def merge(self, A, B):
        out = []
        if not A:
            out = list(B)
        else:
            r = len(B) / len(A)
            acc, bi = 0.0, 0
            for a in A:
                out.append(a)
                acc += r
                while bi < len(B) and acc >= 1.0:
                    out.append(B[bi])
                    bi += 1
                    acc -= 1.0
            out.extend(B[bi:])
        for a in out:
            if len(a) > 7:
                self.tag = a[7]
                self.mmw = a[8]
            self.op(*a[:7])

    def op(self, eng, fn, reads=(), writes=(), sem=None, incn=1, extra=()):
        if self.buf is not None:
            self.buf.append((eng, fn, tuple(reads), tuple(writes), sem,
                             incn, tuple(extra), self.tag,
                             getattr(self, "mmw", 0)))
            self.mmw = 0
            return
        self.meta.append((eng, self.tag, getattr(self, "mmw", 0)))
        self.mmw = 0
        w = self._edge_waits(reads, writes)
        for s, c in extra:
            if w.get(s, 0) < c:
                w[s] = c
        sem = sem or ESEM[eng]
        if sem in ("dX", "dG"):
            slot = self.ring[sem]
            self.ring[sem] = (slot + 1) % 4
            sem = sem + str(slot)
            if w.get(sem, 0) < self.cnt[sem]:
                w[sem] = self.cnt[sem]
        self.cnt[sem] += incn
        cnt = self.cnt[sem]
        seen = self.seen.setdefault(eng, {})
        waits = []
        for s, c in w.items():
            if c > 0 and seen.get(s, -1) < c:
                waits.append((s, c))
                seen[s] = c
        self.items.append((eng, fn, waits, [(sem, incn)]))
        for k in writes:
            self.wr[k] = (sem, cnt)
            self.rd[k] = {}
        for k in reads:
            d = self.rd.setdefault(k, {})
            if d.get(sem, 0) < cnt:
                d[sem] = cnt

    def c(self, sem):
        return self.cnt[sem]

    def play(self, eng_obj, name, sems):
        for eng, fn, waits, incs in self.items:
            if eng != name:
                continue
            for s, v in waits:
                eng_obj.wait_ge(sems[s], v)
            inst = fn(eng_obj)
            for s, n in incs:
                inst.then_inc(sems[s], n)


def build_program():
    nc = bass.Bass("TRN2", target_bir_lowering=False)
    S = Sched()

    DI = lambda n, sh: nc.dram_tensor(n, sh, F32, kind="ExternalInput").ap()
    d_x = DI("x", [L, D]); d_xh = DI("xh", [HALF, D])
    d_wq = DI("wq", [128, 8, 512]); d_wk = DI("wk", [128, 8, 512])
    d_wv = DI("wv", [128, 8, 512]); d_wg = DI("wg", [128, 8, 6])
    d_cvw = DI("cvw", [128, 12, 4]); d_cmx = DI("cmx", [2, 4])
    d_mb2 = DI("mb2", [2, 1]); d_rw = DI("rw", [128, 256])
    d_idm = DI("idm", [128, 128]); d_mLT = DI("mLT", [128, 128])
    d_mL = DI("mL", [128, 128]); d_mGL = DI("mGL", [128, 128])
    d_mGU = DI("mGU", [128, 128]); d_mAT = DI("mAT", [128, 128])
    d_wo = DI("wo", [128, 8, 1024]); d_fw = DI("fw", [128, 8, 1024])
    d_fb = DI("fb", [128, 8])
    d_out = nc.dram_tensor("out", [1024, HALF], F32, kind="ExternalOutput").ap()
    d_msk = DI("msk", [1, 16])
    d_oall = nc.dram_tensor("oall", [2, HALF, 512], F32).ap()
    d_agi = nc.dram_tensor("agi", [HALF, 512], F32).ap()
    d_ago = nc.dram_tensor("ago", [8, HALF, 512], F32).ap()
    d_mxi = nc.dram_tensor("mxi", [1, 16], F32).ap()
    d_mxo = nc.dram_tensor("mxo", [1, 16], F32).ap()

    OFF = [18432]

    def SB(name, shape, dt=F32):
        nb = int(np.prod(shape[1:])) * mybir.dt.size(dt)
        nb = (nb + 31) // 32 * 32
        h = nc.alloc_sbuf_tensor_at(name, shape, dt, offset=OFF[0])
        OFF[0] += nb
        return h.ap()

    ident = SB("ident", [128, 128]); ones1 = SB("ones1", [1, 128])
    mLT = SB("mLT", [128, 128]); mL = SB("mL", [128, 128])
    mGL = SB("mGL", [128, 128]); mGU = SB("mGU", [128, 128])
    mAT = SB("mAT", [128, 128]); rmswt = SB("rmswt", [128, 256])
    cvw = SB("cvw", [128, 12, 4]); cmx = SB("cmx", [2, 4]); mb2 = SB("mb2", [2, 1])
    cols = SB("cols", [128, 16])
    colw = SB("colw", [128, 16])
    colsP = SB("colsP", [128, 2, NSUB, 2])
    scal = SB("scal", [1, 16]); bcc = SB("bcc", [128, 8])
    ecar = SB("ecar", [128, 4]); S0sb = SB("S0sb", [128, 2, 2, 256])
    rmx = SB("rmx", [128, 2])
    epsc = SB("epsc", [128, 2])
    MARK = OFF[0]

    wqs = SB("wqs", [128, 8, 512]); wks = SB("wks", [128, 8, 512])
    wvs = SB("wvs", [128, 8, 512]); wgs = SB("wgs", [128, 8, 6])
    xt = SB("xt", [128, 2, 1024]); xnb = SB("xnb", [128, 1, 1024])
    scr = SB("scr", [128, 256]); xnT = SB("xnT", [128, 2, 8, MAC])
    proj = SB("proj", [128, 2, 12, MAC + 3]); pcv = SB("pcv", [128, 2, 12, MAC])
    gtm = SB("gtm", [128, NSUB, 6]); grow = SB("grow", [2, MAC + 3])
    gbc = SB("gbc", [128, MAC]); geb = SB("geb", [128, MAC])
    gam2 = SB("gam2", [1, 2, 2 * MAC])
    betac = SB("betac", [128, 2, NSUB, 2]); mixc = SB("mixc", [128, 2, NSUB, 2])
    kTM = SB("kTM", [128, 2, NSUB, 256]); KbTM = SB("KbTM", [128, 2, NSUB, 256])
    VbTM = SB("VbTM", [128, 2, NSUB, 256])
    qsc = scr
    khT = SB("khT", [128, 2, 2, MAC]); kbT = SB("kbT", [128, 2, 2, MAC])
    LTs = SB("LTs", [128, 2, 128]); Ls = SB("Ls", [128, 2, 128])
    LT2 = SB("LT2", [128, 2, 128]); L2 = SB("L2", [128, 2, 128])
    LT4 = SB("LT4", [128, 2, 128]); L4 = SB("L4", [128, 2, 128])
    LT8 = SB("LT8", [128, 2, 128]); L8 = SB("L8", [128, 2, 128])
    LT16 = SB("LT16", [128, 2, 128]); Rb = SB("Rb", [128, 2, 2, 128])
    MTs = SB("MTs", [128, 2, NSUB, 128])
    WpTn = SB("WpTn", [128, 2, NSUB, 2, 128])
    Gns = SB("Gns", [128, 2, NSUB, 128]); GTn = SB("GTn", [128, 2, NSUB, 128])
    GT2s = SB("GT2s", [128, 2, NSUB, 128]); ATm = SB("ATm", [128, 2, 128])
    byu = SB("byu", [128, 2, NSUB, 256])
    ueb = SB("ueb", [128, MAC]); esc = SB("esc", [128, MAC])
    emaTM = SB("emaTM", [128, 2, NSUB, 256])
    occ = SB("occ", [128, 2, 256]); oft = SB("oft", [128, 2, 256])
    assert OFF[0] <= 229344, f"phase1 sbuf overflow {OFF[0]}"

    OFF[0] = MARK
    fbuf = SB("fbuf", [128, 8, 1024]); qwb = SB("qwb", [128, 8, 1024], BF16)
    wob = SB("wob", [128, 8, 1024]); xt2 = SB("xt2", [128, 2, 1024])
    xn2 = SB("xn2", [128, 1024]); z0b = SB("z0b", [128, 1024])
    zwb = SB("zwb", [128, 1024]); scr2 = SB("scr2", [128, 1024])
    qib = SB("qib", [128, 1024]); qiT = SB("qiT", [128, 8, 512], BF16)
    otm = SB("otm", [128, 2, 512]); oT = SB("oT", [128, 8, 512])
    gbuf = SB("gbuf", [128, 8, 512]); xTb = SB("xTb", [128, 8, 512])
    fbc = SB("fbc", [128, 8]); outb = SB("outb", [128, 2, 512])
    mrw = SB("mrw", [1, 16]); mcl = SB("mcl", [128, 16])
    tcand = SB("tcand", [128, 2, 512]); town = SB("town", [128, 2, 512])
    tpar = SB("tpar", [128, 512])
    xTb2 = nc.alloc_sbuf_tensor_at("xTb2", [128, 8, 512], F32,
                                   offset=MARK).ap()
    qiT2 = nc.alloc_sbuf_tensor_at("qiT2", [128, 8, 512], BF16,
                                   offset=MARK + 16384).ap()
    assert OFF[0] <= 229344, f"phase2 sbuf overflow {OFF[0]}"

    PSn = lambda n: nc.alloc_psum_tensor(n, [128, 512], F32).ap()
    pS0, pS1 = PSn("pS0"), PSn("pS1")
    pBa, pBb = PSn("pBa"), PSn("pBb")
    pCa, pCb = PSn("pCa"), PSn("pCb")
    pOa, pOb = PSn("pOa"), PSn("pOb")
    BK = {"S0": pS0, "S1": pS1, "Ba": pBa, "Bb": pBb,
          "Ca": pCa, "Cb": pCb, "Oa": pOa, "Ob": pOb}
    import itertools as _it
    TROT = _it.cycle(["Oa", "Ob", "Ba", "Bb"])

    RGALL = (0, 1, 2, 3)

    def mm(bk, out, lhsT, rhs, start, stop, reads=(), extra=(), rg=RGALL):
        S.mmw = int(out.shape[-1])
        S.op("tensor",
             lambda e, o=out, l=lhsT, r=rhs, s1=start, s2=stop:
             e.matmul(o, l, r, start=s1, stop=s2),
             reads=list(reads), writes=["bk:" + bk], extra=extra)

    def tr(bk, out, in_, reads=(), extra=(), rg=RGALL):
        idp = ident[0:in_.shape[0], 0:in_.shape[0]]
        S.mmw = -int(in_.shape[-1])
        S.op("tensor", lambda e, o=out, i=in_, idm=idp: e.transpose(o, i, idm),
             reads=list(reads) + ["ident"],
             writes=["bk:" + bk], extra=extra)

    def ev(eng, bk, fn, reads=(), writes=(), extra=(), rg=RGALL):
        S.op(eng, fn, reads=list(reads) + ["bk:" + bk],
             writes=list(writes), extra=extra)

    def vo(fn, reads=(), writes=(), extra=()):
        S.op("vector", fn, reads=reads, writes=writes, extra=extra)

    def co(fn, reads=(), writes=(), extra=()):
        S.op("scalar", fn, reads=reads, writes=writes, extra=extra)

    def dma(eng, out, in_, reads=(), writes=(), extra=()):
        sem = "dG" if eng == "gpsimd" else "dX"
        S.op(eng, lambda e, o=out, i=in_: e.dma_start(out=o, in_=i),
             reads=reads, writes=writes, sem=sem, incn=16, extra=extra)

    for key, out, src in [("ident", ident[:], d_idm), ("mLT", mLT[:], d_mLT),
                          ("mL", mL[:], d_mL), ("mGL", mGL[:], d_mGL),
                          ("mGU", mGU[:], d_mGU), ("mAT", mAT[:], d_mAT),
                          ("rmswt", rmswt[:], d_rw), ("cvw", cvw[:], d_cvw),
                          ("cmx", cmx[:], d_cmx), ("mb2", mb2[:], d_mb2),
                          ("wqs", wqs[:], d_wq), ("wks", wks[:], d_wk),
                          ("wvs", wvs[:], d_wv), ("wgs", wgs[:], d_wg)]:
        dma("sync", out, src, writes=[key])
    vo(lambda e: e.memset(ones1[:], 1.0), writes=["ones1"])
    vo(lambda e: e.memset(epsc[:, 0:1], EPS_LN), writes=["epsc"])
    vo(lambda e: e.memset(epsc[:, 1:2], EPS_L2), writes=["epsc"])
    vo(lambda e: e.memset(rmx[:], 0.0), writes=["rmx"])
    vo(lambda e: e.memset(ecar[:], 0.0), writes=["ecar"])
    vo(lambda e: e.memset(S0sb[:].rearrange("p a b c -> p (a b c)"), 0.0),
       writes=["S0sb:0", "S0sb:1"])
    vo(lambda e: e.memset(proj[:].rearrange("p a b c -> p (a b c)"), 0.0),
       writes=[f"proj{fam}:{b}:{j}" for fam in ("h", "d")
               for b in range(2) for j in range(12)])
    vo(lambda e: e.memset(grow[:], 0.0), writes=["grow"])

    FSLOT = _it.cycle(["Oa", "Ob"])
    btr_state = [0]
    btr_cur = {"Ba": 0, "Bb": 0}

    def btr(n):
        bank = ("Ba", "Bb")[btr_state[0] % 2]
        btr_state[0] += 1
        if n == 4:
            return bank, 0
        r0 = btr_cur[bank]
        btr_cur[bank] = 2 - r0
        return bank, r0

    def front(m):
        t0 = m * MAC
        bf = m % 2
        S.tag = "front-ln"
        for sub in range(NSUB):
            gi = m * NSUB + sub
            slx = xt[:, gi % 2, :]
            kx = f"xt:{gi % 2}"
            dma("sync", slx, d_x[t0 + sub * 128:t0 + sub * 128 + 128, :],
                writes=[kx])
            c4 = sub * 4
            vo(lambda e, s=slx, c=c4: e.reduce_sum(cols[:, c:c + 1], s,
                                                   axis=AXL.X),
               reads=[kx], writes=[f"cols:{c4}"])
            xs = xnb[:, 0, :]
            kxn = "xn"
            co(lambda e, c=c4: e.activation(cols[:, c + 1:c + 2],
                                            cols[:, c:c + 1], AFT.Copy,
                                            scale=-1.0 / D),
               reads=[f"cols:{c4}"], writes=[f"cols:{c4 + 1}"])
            co(lambda e, s=slx, x_=xs, c=c4: e.activation(
                x_, s, AFT.Square, bias=cols[:, c + 1:c + 2],
                accum_out=cols[:, c + 2:c + 3]),
               reads=[kx, f"cols:{c4 + 1}"], writes=[kxn, f"cols:{c4 + 2}"])
            co(lambda e, c=c4: e.activation(cols[:, c + 3:c + 4],
                                            cols[:, c + 2:c + 3], AFT.Sqrt,
                                            scale=1.0 / D, bias=epsc[:, 0:1]),
               reads=[f"cols:{c4 + 2}", "epsc"], writes=[f"cols:{c4 + 3}"])
            vo(lambda e, c=c4: e.reciprocal(cols[:, c + 3:c + 4],
                                            cols[:, c + 3:c + 4]),
               reads=[f"cols:{c4 + 3}"], writes=[f"cols:{c4 + 3}"])
            vo(lambda e, s=slx, x_=xs, c=c4: e.tensor_scalar(
                x_, s, cols[:, c + 1:c + 2], cols[:, c + 3:c + 4],
                ALU.add, ALU.mult),
               reads=[kx, f"cols:{c4 + 1}", f"cols:{c4 + 3}"], writes=[kxn])
            vo(lambda e, x_=xs: e.reduce_max(rmx[:, 1:2], x_, axis=AXL.X,
                                             apply_absolute_value=True),
               reads=[kxn], writes=["rmx1"])
            vo(lambda e: e.tensor_max(rmx[:, 0:1], rmx[:, 0:1], rmx[:, 1:2]),
               reads=["rmx1"], writes=["rmx"])
            S.tag = "front-xnT"
            for half in (range(2) if DBG_XNT else []):
                bk = next(FSLOT)
                for j in range(4):
                    db = half * 4 + j
                    tr(bk, BK[bk][:, j * 128:(j + 1) * 128],
                       xs[:, db * 128:(db + 1) * 128], reads=[kxn])
                dst = xnT[:, bf, half * 4:half * 4 + 4,
                          sub * 128:(sub + 1) * 128]
                ev("scalar", bk,
                   lambda e, d=dst, b=BK[bk]: CP(
                       e, d, b[:].rearrange("p (j t) -> p j t", j=4)),
                   writes=[f"xnT:{bf}:{half * 4 + j}:{sub}"
                           for j in range(4)])

        if DBG_LEVEL < 2:
            return

        S.tag = "front-proj"
        for dk in range(12):
            bk = next(FSLOT)
            wsrc, wkey = ((wqs, "wqs"), (wks, "wks"), (wvs, "wvs"))[dk // 4]
            dkl = (dk % 4) * 128
            for db in range(8):
                mm(bk, BK[bk][:, 0:MAC], wsrc[:, db, dkl:dkl + 128],
                   xnT[:, bf, db, :],
                   start=(db == 0), stop=(db == 7),
                   reads=[wkey] + [f"xnT:{bf}:{db}:{s}" for s in range(NSUB)])
            dst = proj[:, bf, dk, 3:MAC + 3]
            ev("scalar", bk,
               lambda e, d=dst, b=BK[bk]: CP(e, d, b[:, 0:MAC]),
               writes=[f"projd:{bf}:{dk}"])

        if DBG_LEVEL < 3:
            return
        S.tag = "front-gates"
        for sub in range(NSUB):
            off = sub * 8
            for db in range(8):
                mm("Oa", pOa[:, off:off + 6],
                   xnT[:, bf, db, sub * 128:(sub + 1) * 128], wgs[:, db, :],
                   start=(db == 0), stop=(db == 7),
                   reads=["wgs", f"xnT:{bf}:{db}:{sub}"])
            ev("scalar", "Oa",
               lambda e, s=sub, o=off: e.copy(gtm[:, s, :], pOa[:, o:o + 6]),
               writes=[f"gtm:{sub}"])
            co(lambda e, s=sub: e.activation(betac[:, bf, s, :],
                                             gtm[:, s, 0:2], AFT.Sigmoid),
               reads=[f"gtm:{sub}"], writes=[f"betac:{bf}:{sub}"])
        for sub in range(NSUB):
            tr("Ob", pOb[0:2, sub * 128:(sub + 1) * 128],
               gtm[:, sub, 4:6], reads=[f"gtm:{sub}"])
            tr("Ob", pOb[0:1, 256 + sub * 128:256 + (sub + 1) * 128],
               gtm[:, sub, 2:3], reads=[f"gtm:{sub}"])
        ev("scalar", "Ob", lambda e: e.copy(grow[:, 3:MAC + 3],
                                            pOb[0:2, 0:MAC]),
           reads=["grow"], writes=["growd"])
        ev("scalar", "Ob", lambda e: e.activation(
            gam2[0:1, bf, 0:MAC], pOb[0:1, 256:256 + MAC], AFT.Sigmoid),
           writes=[f"gam2:{bf}"])
        for sub in range(NSUB):
            tr("Oa", pOa[0:1, sub * 128:(sub + 1) * 128],
               gtm[:, sub, 3:4], reads=[f"gtm:{sub}"])
        ev("scalar", "Oa", lambda e: e.activation(
            gam2[0:1, bf, MAC:2 * MAC], pOa[0:1, 0:MAC], AFT.Sigmoid),
           writes=[f"gam2:{bf}"])
        vo(lambda e: e.tensor_scalar(gbc[0:2, 0:MAC], grow[0:2, 0:MAC],
                                     cmx[:, 0:1], None, ALU.mult),
           reads=["growd", "grow", "cmx"], writes=["gbc"])
        for k in range(1, 4):
            vo(lambda e, kk=k: e.scalar_tensor_tensor(
                gbc[0:2, 0:MAC], grow[0:2, kk:kk + MAC], cmx[:, kk:kk + 1],
                gbc[0:2, 0:MAC], ALU.mult, ALU.add),
               reads=["growd", "grow", "cmx", "gbc"], writes=["gbc"])
        co(lambda e: e.activation(gbc[0:2, 0:MAC], gbc[0:2, 0:MAC], AFT.Silu),
           reads=["gbc"], writes=["gbc"])
        co(lambda e: e.activation(gbc[0:2, 0:MAC], gbc[0:2, 0:MAC],
                                  AFT.Sigmoid, bias=mb2[:]),
           reads=["gbc", "mb2"], writes=["gbc"])
        co(lambda e: e.copy(grow[:, 0:3], grow[:, MAC:MAC + 3]),
           reads=["growd"], writes=["grow"])
        for sub in range(NSUB):
            tr("Ob", pOb[0:128, sub * 128:sub * 128 + 2],
               gbc[0:2, sub * 128:(sub + 1) * 128], reads=["gbc"])
        for sub in range(NSUB):
            ev("scalar", "Ob",
               lambda e, s=sub: e.copy(mixc[:, bf, s, :],
                                       pOb[0:128, s * 128:s * 128 + 2]),
               writes=[f"mixc:{bf}:{sub}"])

        if DBG_LEVEL < 4:
            return
        S.tag = "front-conv"
        for j in range(12):
            kj = [f"projd:{bf}:{j}", f"projh:{bf}:{j}", "cvw"]
            vo(lambda e, jj=j: e.tensor_scalar(
                pcv[:, bf, jj, :], proj[:, bf, jj, 0:MAC], cvw[:, jj, 0:1],
                None, ALU.mult), reads=kj, writes=[f"pcv:{bf}:{j}"])
            for k in range(1, 4):
                vo(lambda e, jj=j, kk=k: e.scalar_tensor_tensor(
                    pcv[:, bf, jj, :], proj[:, bf, jj, kk:kk + MAC],
                    cvw[:, jj, kk:kk + 1], pcv[:, bf, jj, :],
                    ALU.mult, ALU.add),
                   reads=kj + [f"pcv:{bf}:{j}"], writes=[f"pcv:{bf}:{j}"])
            vo(lambda e, jj=j: e.tensor_copy(proj[:, 1 - bf, jj, 0:3],
                                             proj[:, bf, jj, MAC:MAC + 3]),
               reads=[f"projd:{bf}:{j}"], writes=[f"projh:{1 - bf}:{j}"])
            for si in range(2 if j < 8 else 1):
                co(lambda e, jj=j: e.activation(pcv[:, bf, jj, :],
                                                pcv[:, bf, jj, :], AFT.Silu),
                   reads=[f"pcv:{bf}:{j}"], writes=[f"pcv:{bf}:{j}"])

    def back(m):
        bf = m % 2
        t0 = m * MAC
        if DBG_LEVEL < 5:
            return
        if DBG_LEVEL < 6:
            return
        S.tag = "back-tm"
        for h in range(H2):
            for sub in range(NSUB):
                bk, r0 = btr(4)
                for idx, j in enumerate([h * 2, h * 2 + 1,
                                         4 + h * 2, 4 + h * 2 + 1]):
                    tr(bk, BK[bk][:, idx * 128:(idx + 1) * 128],
                       pcv[:, bf, j, sub * 128:(sub + 1) * 128],
                       reads=[f"pcv:{bf}:{j}"], rg=(idx,))
                ev("vector", bk, lambda e, b=BK[bk]: e.tensor_copy(
                    qsc[:], b[:, 0:256]), writes=["scr"], rg=(0, 1))
                ev("vector", bk, lambda e, b=BK[bk], hh=h, ss=sub:
                   e.tensor_copy(kTM[:, hh, ss, :], b[:, 256:512]),
                   writes=[f"kTM:{h}:{sub}"], rg=(2, 3))
                bk, r0 = btr(2)
                for idx, j in enumerate([8 + h * 2, 8 + h * 2 + 1]):
                    tr(bk, BK[bk][:, (r0 + idx) * 128:(r0 + idx + 1) * 128],
                       pcv[:, bf, j, sub * 128:(sub + 1) * 128],
                       reads=[f"pcv:{bf}:{j}"], rg=(r0 + idx,))
                ev("vector", bk, lambda e, b=BK[bk], hh=h, ss=sub, r=r0:
                   e.tensor_copy(VbTM[:, hh, ss, :],
                                 b[:, r * 128:(r + 2) * 128]),
                   writes=[f"VbTM:{h}:{sub}"], rg=(r0, r0 + 1))
                co(lambda e: e.activation(scr[:, 0:256], qsc[:], AFT.Square,
                                          accum_out=colw[:, 0:1]),
                   reads=["scr"], writes=["scr", "colw0"])
                co(lambda e: e.activation(colw[:, 1:2], colw[:, 0:1],
                                          AFT.Sqrt, bias=epsc[:, 1:2]),
                   reads=["colw0", "epsc"], writes=["colw1"])
                vo(lambda e: e.reciprocal(colw[:, 1:2], colw[:, 1:2]),
                   reads=["colw1"], writes=["colw1"])
                co(lambda e, hh=h, ss=sub: e.activation(
                    scr[:, 0:256], kTM[:, hh, ss, :], AFT.Square,
                    accum_out=colw[:, 2:3]),
                   reads=[f"kTM:{h}:{sub}"], writes=["scr", "colw2"])
                co(lambda e: e.activation(colw[:, 3:4], colw[:, 2:3],
                                          AFT.Sqrt, bias=epsc[:, 1:2]),
                   reads=["colw2", "epsc"], writes=["colw3"])
                vo(lambda e: e.reciprocal(colw[:, 3:4], colw[:, 3:4]),
                   reads=["colw3"], writes=["colw3"])
                vo(lambda e, hh=h, ss=sub: e.tensor_mul(
                    colw[:, 4:5], colw[:, 3:4], betac[:, bf, ss, hh:hh + 1]),
                   reads=["colw3", f"betac:{bf}:{sub}"], writes=["colw4"])
                vo(lambda e, hh=h, ss=sub: e.tensor_scalar(
                    KbTM[:, hh, ss, :], kTM[:, hh, ss, :], colw[:, 4:5],
                    None, ALU.mult),
                   reads=[f"kTM:{h}:{sub}", "colw4"],
                   writes=[f"KbTM:{h}:{sub}"])
                vo(lambda e, hh=h, ss=sub: e.tensor_scalar(
                    kTM[:, hh, ss, :], kTM[:, hh, ss, :], colw[:, 3:4],
                    None, ALU.mult),
                   reads=[f"kTM:{h}:{sub}", "colw3"],
                   writes=[f"kTM:{h}:{sub}"])
                vo(lambda e, hh=h, ss=sub: e.tensor_scalar(
                    VbTM[:, hh, ss, :], VbTM[:, hh, ss, :],
                    betac[:, bf, ss, hh:hh + 1], None, ALU.mult),
                   reads=[f"VbTM:{h}:{sub}", f"betac:{bf}:{sub}"],
                   writes=[f"VbTM:{h}:{sub}"])
                vo(lambda e, hh=h, ss=sub: e.tensor_scalar(
                    colw[:, 5:6], mixc[:, bf, ss, hh:hh + 1], -1.0, 1.0,
                    ALU.mult, ALU.add),
                   reads=[f"mixc:{bf}:{sub}"], writes=["colw5"])
                vo(lambda e: e.tensor_mul(colw[:, 6:7], colw[:, 1:2],
                                          colw[:, 5:6]),
                   reads=["colw1", "colw5"], writes=["colw6"])
                vo(lambda e, hh=h, ss=sub: e.tensor_copy(
                    colsP[:, hh, ss, 0:1], colw[:, 6:7]),
                   reads=["colw6"], writes=[f"colsP:{h}:{sub}"])
                vo(lambda e, hh=h, ss=sub: e.tensor_copy(
                    colsP[:, hh, ss, 1:2], mixc[:, bf, ss, hh:hh + 1]),
                   reads=[f"mixc:{bf}:{sub}"], writes=[f"colsP:{h}:{sub}"])
                bk, r0 = btr(4)
                for idx, srcS in enumerate([
                        kTM[:, h, sub, 0:128], kTM[:, h, sub, 128:256],
                        KbTM[:, h, sub, 0:128], KbTM[:, h, sub, 128:256]]):
                    tr(bk, BK[bk][:, idx * 128:(idx + 1) * 128], srcS,
                       reads=[f"kTM:{h}:{sub}" if idx < 2 else
                              f"KbTM:{h}:{sub}"], rg=(idx,))
                ev("vector", bk, lambda e, hh=h, ss=sub, b=BK[bk]: CP(
                    e, khT[:, hh, :, ss * 128:(ss + 1) * 128],
                    b[:, 0:256].rearrange("p (j t) -> p j t", j=2)),
                   writes=[f"khT:{h}:{sub}"], rg=(0, 1))
                ev("vector", bk, lambda e, hh=h, ss=sub, b=BK[bk]: CP(
                    e, kbT[:, hh, :, ss * 128:(ss + 1) * 128],
                    b[:, 256:512].rearrange("p (j t) -> p j t", j=2)),
                   writes=[f"kbT:{h}:{sub}"], rg=(2, 3))

        if DBG_LEVEL < 7:
            return
        S.tag = "back-A"
        for sc in range(NSUB):
            for h in range(H2):
                par = h
                bk = "Ca" if par == 0 else "Cb"
                b = BK[bk]
                kh = [f"khT:{h}:{s}" for s in range(NSUB)]
                kb_ = [f"kbT:{h}:{s}" for s in range(NSUB)]
                for blk in range(2):
                    mm(bk, b[:, 0:128],
                       khT[:, h, blk, sc * 128:(sc + 1) * 128],
                       kbT[:, h, blk, sc * 128:(sc + 1) * 128],
                       start=(blk == 0), stop=(blk == 1), reads=kh + kb_,
                       rg=(0,))
                for blk in range(2):
                    mm(bk, b[:, 128:256],
                       kbT[:, h, blk, sc * 128:(sc + 1) * 128],
                       khT[:, h, blk, sc * 128:(sc + 1) * 128],
                       start=(blk == 0), stop=(blk == 1), reads=kh + kb_,
                       rg=(1,))
                ev("vector", bk, lambda e, p=par, b_=b: e.tensor_mul(
                    LTs[:, p, :], b_[:, 0:128], mLT[:]),
                   reads=["mLT"], writes=[f"LTs:{par}"], rg=(0,))
                ev("vector", bk, lambda e, p=par, b_=b: e.tensor_mul(
                    Ls[:, p, :], b_[:, 128:256], mL[:]),
                   reads=["mL"], writes=[f"Ls:{par}"], rg=(1,))
                sq = [(Ls, "Ls", LTs, "LTs", LT2, "LT2"),
                      (LTs, "LTs", Ls, "Ls", L2, "L2"),
                      (L2, "L2", LT2, "LT2", LT4, "LT4"),
                      (LT2, "LT2", L2, "L2", L4, "L4"),
                      (L4, "L4", LT4, "LT4", LT8, "LT8"),
                      (LT4, "LT4", L4, "L4", L8, "L8"),
                      (L8, "L8", LT8, "LT8", LT16, "LT16")]
                for a, ka, bb_, kbb, dstT, kd in sq:
                    mm(bk, b[:, 256:384], a[:, par, :], bb_[:, par, :],
                       start=True, stop=True,
                       reads=[f"{ka}:{par}", f"{kbb}:{par}"], rg=(2,))
                    ev("vector", bk, lambda e, d=dstT, p=par, b_=b:
                       e.tensor_copy(d[:, p, :], b_[:, 256:384]),
                       writes=[f"{kd}:{par}"], rg=(2,))
                vo(lambda e, p=par: e.tensor_add(Rb[:, p, 0, :],
                                                 LT16[:, p, :], ident[:]),
                   reads=[f"LT16:{par}", "ident"], writes=[f"Rb:{par}:0"])
                cur = 0
                for pw, kpw in [(L8, "L8"), (L4, "L4"), (L2, "L2")]:
                    mm(bk, b[:, 384:512], pw[:, par, :], Rb[:, par, cur, :],
                       start=True, stop=True,
                       reads=[f"{kpw}:{par}", f"Rb:{par}:{cur}"], rg=(3,))
                    ev("vector", bk, lambda e, p=par, c=cur, b_=b:
                       e.tensor_add(Rb[:, p, 1 - c, :], b_[:, 384:512],
                                    Rb[:, p, c, :]),
                       reads=[f"Rb:{par}:{cur}"],
                       writes=[f"Rb:{par}:{1 - cur}"], rg=(3,))
                    cur = 1 - cur
                mm(bk, b[:, 384:512], Ls[:, par, :], Rb[:, par, cur, :],
                   start=True, stop=True,
                   reads=[f"Ls:{par}", f"Rb:{par}:{cur}"], rg=(3,))
                ev("vector", bk, lambda e, p=par, c=cur, hh=h, ss=sc, b_=b:
                   e.tensor_sub(MTs[:, hh, ss, :], Rb[:, p, c, :],
                                b_[:, 384:512]),
                   reads=[f"Rb:{par}:{cur}"], writes=[f"MTs:{h}:{sc}"],
                   rg=(3,))

        S.tag = "back-ema"
        for h in range(H2):
            bk = "S0" if h == 0 else "S1"
            mm(bk, BK[bk][:, 0:MAC], ones1[:],
               gam2[0:1, bf, h * MAC:(h + 1) * MAC], start=True,
               stop=True, reads=["ones1", f"gam2:{bf}"], rg=(0, 1))
            ev("vector", bk, lambda e, b=BK[bk]: e.tensor_copy(geb[:],
                                                               b[:, 0:MAC]),
               writes=["geb"], rg=(0, 1))
            for blk in range(2):
                j = 8 + h * 2 + blk
                vo(lambda e, jj=j: e.tensor_mul(ueb[:], geb[:],
                                                pcv[:, bf, jj, :]),
                   reads=["geb", f"pcv:{bf}:{j}"], writes=["ueb"])
                vo(lambda e, jj=j: e.tensor_sub(ueb[:], pcv[:, bf, jj, :],
                                                ueb[:]),
                   reads=["ueb", f"pcv:{bf}:{j}"], writes=["ueb"])
                vo(lambda e, bb=blk, hh=h: e.tensor_tensor_scan(
                    esc[:], geb[:], ueb[:],
                    ecar[:, hh * 2 + bb:hh * 2 + bb + 1], ALU.mult, ALU.add),
                   reads=["geb", "ueb", "ecar"], writes=["esc"])
                vo(lambda e, bb=blk, hh=h: e.tensor_copy(
                    ecar[:, hh * 2 + bb:hh * 2 + bb + 1],
                    esc[:, MAC - 1:MAC]),
                   reads=["esc"], writes=["ecar"])
                bk2, r0 = btr(2)
                for sc in range(NSUB):
                    tr(bk2, BK[bk2][:, (r0 + sc) * 128:(r0 + sc + 1) * 128],
                       esc[:, sc * 128:(sc + 1) * 128], reads=["esc"],
                       rg=(r0 + sc,))
                ev("vector", bk2, lambda e, hh=h, bb=blk, b=BK[bk2], r=r0:
                   e.tensor_copy(
                       emaTM[:, hh, :, bb * 128:bb * 128 + 128],
                       b[:, r * 128:(r + NSUB) * 128].rearrange(
                           "p (s t) -> p s t", s=NSUB)),
                   writes=[f"emaTM:{h}:{sc}" for sc in range(NSUB)],
                   rg=(r0, r0 + 1))

        if DBG_LEVEL < 8:
            return
        S.tag = "back-B"
        for sc in range(NSUB):
            for h in range(H2):
                bkB = "Ba" if h == 0 else "Bb"
                bkG = "Ca" if h == 0 else "Cb"
                bB, bG = BK[bkB], BK[bkG]
                mm(bkB, bB[:, 0:256], MTs[:, h, sc, :], VbTM[:, h, sc, :],
                   start=True, stop=True,
                   reads=[f"MTs:{h}:{sc}", f"VbTM:{h}:{sc}"])
                ev("vector", bkB, lambda e, hh=h, ss=sc, b_=bB: e.tensor_copy(
                    byu[:, hh, ss, :], b_[:, 0:256]),
                   writes=[f"byu:{h}:{sc}"])
                for blk in range(2):
                    mm(bkG, bG[:, 256 + blk * 128:384 + blk * 128],
                       KbTM[:, h, sc, blk * 128:(blk + 1) * 128],
                       MTs[:, h, sc, :], start=True, stop=True,
                       reads=[f"KbTM:{h}:{sc}", f"MTs:{h}:{sc}"])
                ev("vector", bkG, lambda e, hh=h, ss=sc, b_=bG:
                   e.tensor_scalar(
                       WpTn[:, hh, ss, :, :].rearrange("p b t -> p (b t)"),
                       b_[:, 256:512], -1.0, None, ALU.mult),
                   writes=[f"WpTn:{h}:{sc}"])
                for blk in range(2):
                    mm(bkG, bG[:, 0:128], WpTn[:, h, sc, blk, :],
                       khT[:, h, blk, sc * 128:(sc + 1) * 128],
                       start=(blk == 0), stop=(blk == 1),
                       reads=[f"WpTn:{h}:{sc}", f"khT:{h}:{sc}"])
                for blk in range(2):
                    mm(bkG, bG[:, 128:256],
                       khT[:, h, blk, sc * 128:(sc + 1) * 128],
                       WpTn[:, h, sc, blk, :],
                       start=(blk == 0), stop=(blk == 1),
                       reads=[f"WpTn:{h}:{sc}", f"khT:{h}:{sc}"])
                ev("vector", bkG, lambda e, hh=h, ss=sc, b_=bG: e.tensor_mul(
                    Gns[:, hh, ss, :], b_[:, 0:128], mGL[:]),
                   reads=["mGL"], writes=[f"Gns:{h}:{sc}"])
                ev("vector", bkG, lambda e, hh=h, ss=sc, b_=bG: e.tensor_mul(
                    GTn[:, hh, ss, :], b_[:, 128:256], mGU[:]),
                   reads=["mGU"], writes=[f"GTn:{h}:{sc}"])
                mm(bkG, bG[:, 256:384], Gns[:, h, sc, :], GTn[:, h, sc, :],
                   start=True, stop=True,
                   reads=[f"Gns:{h}:{sc}", f"GTn:{h}:{sc}"])
                ev("vector", bkG, lambda e, hh=h, ss=sc, b_=bG: e.tensor_copy(
                    GT2s[:, hh, ss, :], b_[:, 256:384]),
                   writes=[f"GT2s:{h}:{sc}"])
        for sc in range(NSUB):
            for h in range(H2):
                bkB = "Ba" if h == 0 else "Bb"
                bkG = "Ca" if h == 0 else "Cb"
                pS = pS0 if h == 0 else pS1
                bkS = "S0" if h == 0 else "S1"
                bkO, bO = bkS, pS
                bB, bG = BK[bkB], BK[bkG]
                kS0 = f"S0sb:{h}"
                for blk in range(2):
                    mm(bkB, bB[:, 0:256], WpTn[:, h, sc, blk, :],
                       S0sb[:, h, blk, :], start=(blk == 0), stop=(blk == 1),
                       reads=[f"WpTn:{h}:{sc}", kS0])
                ev("vector", bkB, lambda e, hh=h, ss=sc, b_=bB: e.tensor_add(
                    byu[:, hh, ss, :], byu[:, hh, ss, :], b_[:, 0:256]),
                   reads=[f"byu:{h}:{sc}"], writes=[f"byu:{h}:{sc}"])
                mm(bkB, bB[:, 256:512], GT2s[:, h, sc, :], byu[:, h, sc, :],
                   start=True, stop=True,
                   reads=[f"GT2s:{h}:{sc}", f"byu:{h}:{sc}"])
                ev("vector", bkB, lambda e, hh=h, ss=sc, b_=bB: e.tensor_add(
                    byu[:, hh, ss, :], byu[:, hh, ss, :], b_[:, 256:512]),
                   reads=[f"byu:{h}:{sc}"], writes=[f"byu:{h}:{sc}"])
                mm(bkB, bB[:, 0:256], GTn[:, h, sc, :], byu[:, h, sc, :],
                   start=True, stop=True,
                   reads=[f"GTn:{h}:{sc}", f"byu:{h}:{sc}"])
                ev("vector", bkB, lambda e, hh=h, ss=sc, b_=bB: e.tensor_add(
                    byu[:, hh, ss, :], byu[:, hh, ss, :], b_[:, 0:256]),
                   reads=[f"byu:{h}:{sc}"], writes=[f"byu:{h}:{sc}"])
                for blk in range(2):
                    j = h * 2 + blk
                    mm(bkO, bO[:, 0:256],
                       pcv[:, bf, j, sc * 128:(sc + 1) * 128],
                       S0sb[:, h, blk, :],
                       start=(blk == 0), stop=False,
                       reads=[f"pcv:{bf}:{j}", kS0], rg=(0, 1))
                for blk in range(2):
                    j = h * 2 + blk
                    mm(bkG, bG[:, 384:512],
                       khT[:, h, blk, sc * 128:(sc + 1) * 128],
                       pcv[:, bf, j, sc * 128:(sc + 1) * 128],
                       start=(blk == 0), stop=(blk == 1),
                       reads=[f"khT:{h}:{sc}", f"pcv:{bf}:{j}"], rg=(3,))
                ev("vector", bkG, lambda e, hh=h, b_=bG: e.tensor_mul(
                    ATm[:, hh, :], b_[:, 384:512], mAT[:]),
                   reads=["mAT"], writes=[f"ATm:{h}"], rg=(3,))
                mm(bkO, bO[:, 0:256], ATm[:, h, :], byu[:, h, sc, :],
                   start=False, stop=True,
                   reads=[f"ATm:{h}", f"byu:{h}:{sc}"])
                ev("vector", bkO, lambda e, hh=h, ss=sc, b_=bO: e.tensor_scalar(
                    occ[:, hh, :], b_[:, 0:256], colsP[:, hh, ss, 0:1],
                    None, ALU.mult),
                   reads=[f"colsP:{h}:{sc}"], writes=[f"occ:{h}"])
                for blk in range(2):
                    mm(bkS, pS[:, blk * 256:(blk + 1) * 256],
                       kTM[:, h, sc, blk * 128:(blk + 1) * 128],
                       byu[:, h, sc, :],
                       start=True, stop=True,
                       reads=[f"kTM:{h}:{sc}", f"byu:{h}:{sc}"])
                ev("vector", bkS, lambda e, hh=h, p=pS: e.tensor_add(
                    S0sb[:, hh, :, :].rearrange("p b d -> p (b d)"),
                    S0sb[:, hh, :, :].rearrange("p b d -> p (b d)"), p[:]),
                   reads=[kS0], writes=[kS0])
                vo(lambda e, hh=h, ss=sc: e.scalar_tensor_tensor(
                    occ[:, hh, :], emaTM[:, hh, ss, :],
                    colsP[:, hh, ss, 1:2], occ[:, hh, :], ALU.mult, ALU.add),
                   reads=[f"emaTM:{h}:{sc}", f"colsP:{h}:{sc}", f"occ:{h}"],
                   writes=[f"occ:{h}"])
                co(lambda e, hh=h: e.activation(
                    scr[:, 0:256], occ[:, hh, :], AFT.Square,
                    accum_out=colw[:, 8 + hh * 2:9 + hh * 2]),
                   reads=[f"occ:{h}"], writes=["scr", f"colw8:{h}"])
                co(lambda e, hh=h: e.activation(
                    colw[:, 9 + hh * 2:10 + hh * 2],
                    colw[:, 8 + hh * 2:9 + hh * 2], AFT.Sqrt,
                    scale=1.0 / DK, bias=epsc[:, 0:1]),
                   reads=[f"colw8:{h}", "epsc"], writes=[f"colw9:{h}"])
                vo(lambda e, hh=h: e.reciprocal(
                    colw[:, 9 + hh * 2:10 + hh * 2],
                    colw[:, 9 + hh * 2:10 + hh * 2]),
                   reads=[f"colw9:{h}"], writes=[f"colw9:{h}"])
                slot = (sc * 2 + h) % 2
                vo(lambda e, hh=h: e.tensor_scalar(
                    occ[:, hh, :], occ[:, hh, :],
                    colw[:, 9 + hh * 2:10 + hh * 2], None, ALU.mult),
                   reads=[f"occ:{h}", f"colw9:{h}"], writes=[f"occ:{h}"])
                vo(lambda e, hh=h, sl=slot: e.tensor_mul(
                    oft[:, sl, :], occ[:, hh, :], rmswt[:]),
                   reads=[f"occ:{h}", "rmswt"], writes=[f"oft:{slot}"])
                dma("gpsimd",
                    d_oall.rearrange("s t d -> (s t) d")[
                        t0 + sc * 128:t0 + sc * 128 + 128,
                        h * 256:(h + 1) * 256],
                    oft[:, slot, :], reads=[f"oft:{slot}"])

    NM = (DBG_NMAC or NMAC) if DBG_LEVEL >= 1 else 0
    prevB = None
    for m in range(NM):
        S.begin()
        front(m)
        F = S.take()
        if prevB is None:
            S.emit(F)
        else:
            S.merge(prevB, F)
        S.begin()
        back(m)
        prevB = S.take()
    if prevB is not None:
        S.emit(prevB)

    if not DBG_PHASE2:
        if DBG_DUMP:
            dma("gpsimd", d_out.rearrange("a b -> (a b)"),
                d_oall.rearrange("s t d -> (s t d)"),
                extra=[("dG" + str(i), S.c("dG" + str(i))) for i in range(4)])
        return _finish(nc, S)
    tr("Ca", pCa[0:1, 0:128], rmx[:, 0:1], reads=["rmx"])
    ev("vector", "Ca", lambda e: e.tensor_copy(scr[0:1, 0:128],
                                               pCa[0:1, 0:128]),
       writes=["scr"])
    vo(lambda e: e.memset(scal[0:1, 0:16], 0.0), writes=["scal"])
    vo(lambda e: e.reduce_max(scal[0:1, 0:1], scr[0:1, 0:128], axis=AXL.X),
       reads=["scr"], writes=["scal"])
    dma("gpsimd", d_mxi[:, :], scal[0:1, 0:16], reads=["scal"],
        writes=["d_mxi"])
    S.op("gpsimd", lambda e: e.collective_compute(
        "AllReduce", ALU.max, replica_groups=[list(range(8))],
        ins=[d_mxi.opt()], outs=[d_mxo.opt()]),
        reads=["d_mxi"], writes=["d_mxo"])
    dma("gpsimd", scal[0:1, 0:16], d_mxo[:, :], reads=["d_mxo"],
        writes=["scal"])
    dma("sync", mrw[:], d_msk[:, :], writes=["mrw"],
        extra=[("T", S.c("T")), ("V", S.c("V")), ("C", S.c("C"))])
    mm("Cb", pCb[:, 128:144], ones1[:], mrw[:], start=True, stop=True,
       reads=["mrw", "ones1"])
    ev("vector", "Cb", lambda e: e.tensor_copy(mcl[:], pCb[:, 128:144]),
       writes=["mcl"])
    OALL_DG = [("dG" + str(i), S.c("dG" + str(i))) for i in range(4)]
    for t in range(HALF // 128):
        sl = t % 2
        dma("gpsimd", town[:, sl, :], d_oall[0, t * 128:(t + 1) * 128, :],
            writes=[f"town:{sl}"], extra=OALL_DG)
        dma("gpsimd", tpar[:], d_oall[1, t * 128:(t + 1) * 128, :],
            writes=["tpar"], extra=OALL_DG)
        vo(lambda e, s=sl: e.tensor_scalar(town[:, s, :], town[:, s, :],
                                           mcl[:, 3:4], None, ALU.mult),
           reads=[f"town:{sl}", "mcl"], writes=[f"town:{sl}"])
        vo(lambda e, s=sl: e.scalar_tensor_tensor(
            town[:, s, :], tpar[:], mcl[:, 2:3], town[:, s, :],
            ALU.mult, ALU.add),
           reads=["tpar", "mcl", f"town:{sl}"], writes=[f"town:{sl}"])
        dma("gpsimd", d_agi[t * 128:(t + 1) * 128, :], town[:, sl, :],
            reads=[f"town:{sl}"], writes=["d_agi"])
    S.op("gpsimd", lambda e: e.collective_compute(
        "AllGather", ALU.bypass, replica_groups=[list(range(8))],
        ins=[d_agi.opt()], outs=[d_ago.rearrange("s t d -> (s t) d").opt()]),
        reads=["d_agi"], writes=["d_ago"],
        extra=[("dG" + str(i), S.c("dG" + str(i))) for i in range(4)])

    vo(lambda e: e.reciprocal(scal[0:1, 1:2], scal[0:1, 0:1]),
       reads=["scal"], writes=["scal1"])
    vo(lambda e: e.tensor_mul(scal[0:1, 2:3], scal[0:1, 1:2], scal[0:1, 0:1]),
       reads=["scal", "scal1"], writes=["scal2"])
    vo(lambda e: e.tensor_scalar(scal[0:1, 2:3], scal[0:1, 2:3], -1.0, 2.0,
                                 ALU.mult, ALU.add),
       reads=["scal2"], writes=["scal2"])
    vo(lambda e: e.tensor_mul(scal[0:1, 1:2], scal[0:1, 1:2], scal[0:1, 2:3]),
       reads=["scal1", "scal2"], writes=["scal1"])
    vo(lambda e: e.tensor_scalar(scal[0:1, 3:4], scal[0:1, 1:2], 127.0, None,
                                 ALU.mult), reads=["scal1"], writes=["scal3"])

    dma("sync", fbuf[:].rearrange("p a b -> p (a b)"),
        d_fw.rearrange("p a b -> p (a b)"),
        writes=["fbuf"], extra=[("T", S.c("T")), ("V", S.c("V")),
                                ("C", S.c("C"))])
    dma("sync", wob[:].rearrange("p a b -> p (a b)"),
        d_wo.rearrange("p a b -> p (a b)"), writes=["wob"])
    dma("sync", fbc[:], d_fb, writes=["fbc"])
    for kb in range(8):
        co(lambda e, k=kb: e.activation(scr2[:], fbuf[:, k, :], AFT.Abs,
                                        accum_out=colw[:, 12:13]),
           reads=["fbuf"], writes=["scr2", "colw12"])
        if kb == 0:
            vo(lambda e: e.tensor_copy(colw[:, 13:14], colw[:, 12:13]),
               reads=["colw12"], writes=["colw13"])
        else:
            vo(lambda e: e.tensor_add(colw[:, 13:14], colw[:, 13:14],
                                      colw[:, 12:13]),
               reads=["colw12", "colw13"], writes=["colw13"])
    tr("Cb", pCb[0:1, 0:128], colw[:, 13:14], reads=["colw13"])
    ev("vector", "Cb", lambda e: e.tensor_copy(scr2[0:1, 0:128],
                                               pCb[0:1, 0:128]),
       writes=["scr2"])
    vo(lambda e: e.reduce_sum(scal[0:1, 4:5], scr2[0:1, 0:128], axis=AXL.X),
       reads=["scr2"], writes=["scal4"])
    vo(lambda e: e.tensor_scalar(scal[0:1, 5:6], scal[0:1, 4:5],
                                 1.0 / (D * D), None, ALU.mult),
       reads=["scal4"], writes=["scal5"])
    vo(lambda e: e.tensor_scalar(scal[0:1, 6:7], scal[0:1, 5:6], 0.5, None,
                                 ALU.mult), reads=["scal5"], writes=["scal6"])
    vo(lambda e: e.tensor_mul(scal[0:1, 7:8], scal[0:1, 5:6], scal[0:1, 0:1]),
       reads=["scal5", "scal"], writes=["scal7"])
    vo(lambda e: e.tensor_scalar(scal[0:1, 7:8], scal[0:1, 7:8], REC127,
                                 None, ALU.mult),
       reads=["scal7"], writes=["scal7"])
    for i, skey, scol in [(0, "scal3", 3), (1, "scal7", 7), (2, "scal6", 6)]:
        mm("Ca", pCa[:, i * 128:i * 128 + 1], ones1[:],
           scal[0:1, scol:scol + 1], start=True, stop=True,
           reads=["ones1", skey])
    for i in range(3):
        ev("vector", "Ca", lambda e, ii=i: e.tensor_copy(
            bcc[:, ii:ii + 1], pCa[:, ii * 128:ii * 128 + 1]),
           writes=[f"bcc{i}"])
    for kb in range(8):
        co(lambda e, k=kb: e.activation(scr2[:], fbuf[:, k, :], AFT.Abs),
           reads=["fbuf"], writes=["scr2"])
        co(lambda e, k=kb: e.activation(qib[:], fbuf[:, k, :], AFT.Sign),
           reads=["fbuf"], writes=["qib"])
        vo(lambda e: e.tensor_scalar(scr2[:], scr2[:], bcc[:, 2:3], None,
                                     ALU.is_gt),
           reads=["scr2", "bcc2"], writes=["scr2"])
        vo(lambda e, k=kb: e.tensor_mul(qwb[:, k, :], qib[:], scr2[:]),
           reads=["qib", "scr2"], writes=["qwb"])

    if DBG_P2 == "qw":
        for kb in range(8):
            vo(lambda e, k=kb: e.tensor_copy(scr2[:], qwb[:, k, :]),
               reads=["qwb"], writes=["scr2"])
            dma("gpsimd", d_out[kb * 128:(kb + 1) * 128, 0:1024], scr2[:],
                reads=["scr2"])
    P2F = _it.cycle(["Oa", "Ob"])
    P2B = _it.cycle(["Ba", "Bb"])

    def front2(om):
        u0 = om * MAC2
        pb = om % 2
        qiTb, xTbb = (qiT, xTb) if pb == 0 else (qiT2, xTb2)
        exw = [] if pb == 0 else ["fbuf"]
        S.tag = "p2-front"
        for sub in range(NSUB2):
            gi = om * NSUB + sub
            slx = xt2[:, gi % 2, :]
            kx = f"xt2:{gi % 2}"
            dma("sync", slx, d_xh[u0 + sub * 128:u0 + sub * 128 + 128, :],
                writes=[kx])
            vo(lambda e, s=slx: e.reduce_sum(cols[:, 0:1], s, axis=AXL.X),
               reads=[kx], writes=["cols:0"])
            co(lambda e: e.activation(cols[:, 1:2], cols[:, 0:1], AFT.Copy,
                                      scale=-1.0 / D),
               reads=["cols:0"], writes=["cols:1"])
            co(lambda e, s=slx: e.activation(scr2[:], s, AFT.Square,
                                             bias=cols[:, 1:2],
                                             accum_out=cols[:, 2:3]),
               reads=[kx, "cols:1"], writes=["scr2", "cols:2"])
            co(lambda e: e.activation(cols[:, 3:4], cols[:, 2:3], AFT.Sqrt,
                                      scale=1.0 / D, bias=epsc[:, 0:1]),
               reads=["cols:2", "epsc"], writes=["cols:3"])
            vo(lambda e: e.reciprocal(cols[:, 3:4], cols[:, 3:4]),
               reads=["cols:3"], writes=["cols:3"])
            vo(lambda e, s=slx: e.tensor_scalar(
                xn2[:], s, cols[:, 1:2], cols[:, 3:4], ALU.add, ALU.mult),
               reads=[kx, "cols:1", "cols:3"], writes=["xn2"])
            vo(lambda e: e.tensor_scalar(z0b[:], xn2[:], bcc[:, 0:1], 128.5,
                                         ALU.mult, ALU.add),
               reads=["xn2", "bcc0"], writes=["z0b"])
            QK = ["qib0", "qib1"]
            first = True
            for bit in [128.0, 64.0, 32.0, 16.0, 8.0, 4.0, 2.0, 1.0]:
                src, ksrc = (z0b, "z0b") if first else (zwb, "zwb")
                vo(lambda e, b_=bit, s=src: e.tensor_scalar(
                    qib[:], s[:], b_, b_, ALU.is_ge, ALU.mult),
                   reads=[ksrc], writes=QK)
                vo(lambda e, s=src: e.tensor_sub(zwb[:], s[:], qib[:]),
                   reads=[ksrc] + QK, writes=["zwb"])
                first = False
            vo(lambda e: e.tensor_sub(qib[:], z0b[:], zwb[:]),
               reads=["z0b", "zwb"], writes=QK)
            vo(lambda e: e.tensor_scalar(qib[:], qib[:], 128.0, 127.0,
                                         ALU.subtract, ALU.min),
               reads=QK, writes=QK)
            if DBG_P2 == "qi":
                gi2 = om * NSUB2 + sub
                dma("gpsimd", d_out[(gi2 % 8) * 128:(gi2 % 8) * 128 + 128,
                                    (gi2 // 8) * 1024:(gi2 // 8) * 1024 + 1024],
                    qib[:], reads=["qib0", "qib1"])
            for half in range(2):
                bk = next(P2F)
                for j in range(4):
                    db = half * 4 + j
                    tr(bk, BK[bk][:, j * 128:(j + 1) * 128],
                       qib[:, db * 128:(db + 1) * 128],
                       reads=[f"qib{half}"])
                dst = qiTb[:, half * 4:half * 4 + 4,
                           sub * 128:(sub + 1) * 128]
                ev("scalar", bk,
                   lambda e, d=dst, b=BK[bk]: CP(
                       e, d, b[:].rearrange("p (j t) -> p j t", j=4)),
                   writes=[f"qiT:{pb}:{half * 4 + j}" for j in range(4)]
                   + exw)
            for half in range(2):
                bk = next(P2F)
                for j in range(4):
                    db = half * 4 + j
                    tr(bk, BK[bk][:, j * 128:(j + 1) * 128],
                       slx[:, db * 128:(db + 1) * 128], reads=[kx])
                dst = xTbb[:, half * 4:half * 4 + 4,
                           sub * 128:(sub + 1) * 128]
                ev("scalar", bk,
                   lambda e, d=dst, b=BK[bk]: CP(
                       e, d, b[:].rearrange("p (j t) -> p j t", j=4)),
                   writes=[f"xTb:{pb}:{half * 4 + j}" for j in range(4)]
                   + exw)

    def back2(om):
        u0 = om * MAC2
        pb = om % 2
        qiTb, xTbb = (qiT, xTb) if pb == 0 else (qiT2, xTb2)
        S.tag = "p2-ffn"
        for ob_ in range(8):
            bk = "Ca" if ob_ % 2 == 0 else "Cb"
            for kb in range(8):
                mm(bk, BK[bk][:], qwb[:, kb, ob_ * 128:(ob_ + 1) * 128],
                   qiTb[:, kb, :], start=(kb == 0), stop=(kb == 7),
                   reads=["qwb", f"qiT:{pb}:{kb}"])
            ev("scalar", bk, lambda e, o=ob_, b=BK[bk]: e.activation(
                gbuf[:, o, :], b[:], AFT.Silu, scale=bcc[:, 1:2],
                bias=fbc[:, o:o + 1]),
               reads=["bcc1", "fbc"], writes=[f"gbuf:{ob_}"])
            if DBG_P2 == "g":
                dma("gpsimd", d_out[ob_ * 128:(ob_ + 1) * 128, u0:u0 + MAC2],
                    gbuf[:, ob_, :], reads=[f"gbuf:{ob_}"])

        S.tag = "p2-ox"
        for sub in range(NSUB2):
            r0 = u0 + sub * 128
            for sl in range(2):
                dma("sync", town[:, sl, :], d_oall[sl, r0:r0 + 128, :],
                    writes=[f"town:{sl}"])
            dma("sync", tcand[:, 0, :], d_ago[0, r0:r0 + 128, :],
                reads=["d_ago"], writes=["tcand:0"])
            vo(lambda e: e.tensor_scalar(town[:, 0, :], town[:, 0, :],
                                         mcl[:, 2:3], None, ALU.mult),
               reads=["town:0", "mcl"], writes=["town:0"])
            vo(lambda e: e.scalar_tensor_tensor(
                town[:, 0, :], town[:, 1, :], mcl[:, 3:4], town[:, 0, :],
                ALU.mult, ALU.add),
               reads=["town:1", "mcl", "town:0"], writes=["town:0"])
            vo(lambda e: e.tensor_scalar(tpar[:], tcand[:, 0, :],
                                         mcl[:, 4:5], None, ALU.mult),
               reads=["tcand:0", "mcl"], writes=["tpar"])
            for i in range(1, 8):
                sl = i % 2
                dma("sync", tcand[:, sl, :], d_ago[i, r0:r0 + 128, :],
                    reads=["d_ago"], writes=[f"tcand:{sl}"])
                vo(lambda e, ii=i, s=sl: e.scalar_tensor_tensor(
                    tpar[:], tcand[:, s, :], mcl[:, 4 + ii:5 + ii], tpar[:],
                    ALU.mult, ALU.add),
                   reads=[f"tcand:{sl}", "mcl", "tpar"], writes=["tpar"])
            vo(lambda e: e.tensor_scalar(otm[:, 0, :], town[:, 0, :],
                                         mcl[:, 2:3], None, ALU.mult),
               reads=["town:0", "mcl"], writes=["otm:0"])
            vo(lambda e: e.scalar_tensor_tensor(
                otm[:, 0, :], tpar[:], mcl[:, 3:4], otm[:, 0, :],
                ALU.mult, ALU.add),
               reads=["tpar", "mcl", "otm:0"], writes=["otm:0"])
            vo(lambda e: e.tensor_scalar(otm[:, 1, :], town[:, 0, :],
                                         mcl[:, 3:4], None, ALU.mult),
               reads=["town:0", "mcl"], writes=["otm:1"])
            vo(lambda e: e.scalar_tensor_tensor(
                otm[:, 1, :], tpar[:], mcl[:, 2:3], otm[:, 1, :],
                ALU.mult, ALU.add),
               reads=["tpar", "mcl", "otm:1"], writes=["otm:1"])
            for half in range(2):
                bk = next(P2B)
                for j in range(4):
                    tr(bk, BK[bk][:, j * 128:(j + 1) * 128],
                       otm[:, half, j * 128:(j + 1) * 128],
                       reads=[f"otm:{half}"])
                dst = oT[:, half * 4:half * 4 + 4,
                          sub * 128:(sub + 1) * 128]
                ev("vector", bk,
                   lambda e, d=dst, b=BK[bk]: CP(
                       e, d, b[:].rearrange("p (j t) -> p j t", j=4)),
                   writes=[f"oT:{half * 4 + j}" for j in range(4)])
        if DBG_P2 == "oT":
            for db in range(8):
                dma("gpsimd", d_out[db * 128:(db + 1) * 128, u0:u0 + MAC2],
                    oT[:, db, :], reads=[f"oT:{db}"])
        S.tag = "p2-wo"
        for ob_ in range(8):
            bk = "Ba" if ob_ % 2 == 0 else "Bb"
            for kb in range(8):
                mm(bk, BK[bk][:], wob[:, kb, ob_ * 128:(ob_ + 1) * 128],
                   oT[:, kb, :], start=(kb == 0), stop=(kb == 7),
                   reads=["wob"] + [f"oT:{kb}"])
            par = ob_ % 2
            ev("vector", bk, lambda e, o=ob_, b=BK[bk], p=par:
               e.tensor_mul(outb[:, p, :], b[:], gbuf[:, o, :]),
               reads=[f"gbuf:{ob_}"], writes=[f"outb:{par}"])
            vo(lambda e, o=ob_, p=par, xb=xTbb: e.tensor_add(
                outb[:, p, :], outb[:, p, :], xb[:, o, :]),
               reads=[f"outb:{par}", f"xTb:{pb}:{ob_}"],
               writes=[f"outb:{par}"])
            if DBG_P2 is None:
                dma("gpsimd", d_out[ob_ * 128:(ob_ + 1) * 128, u0:u0 + MAC2],
                    outb[:, par, :], reads=[f"outb:{par}"])

    prevB2 = None
    for om in range(NOM):
        S.begin()
        front2(om)
        F2 = S.take()
        if prevB2 is None:
            S.emit(F2)
        else:
            S.merge(prevB2, F2)
        S.begin()
        back2(om)
        prevB2 = S.take()
    if prevB2 is not None:
        S.emit(prevB2)

    global _SCHED
    _SCHED = S
    return _finish(nc, S)


def _finish(nc, S):
    sems = {}
    for name in (["T", "V", "C", "G"]
                 + [q + str(i) for q in ("dX", "dG") for i in range(4)]):
        sems[name] = nc.alloc_semaphore(name)

    with nc.Block() as block:
        @block.sync
        def _(eng):
            S.play(eng, "sync", sems)

        @block.tensor
        def _(eng):
            S.play(eng, "tensor", sems)

        @block.vector
        def _(eng):
            S.play(eng, "vector", sems)

        @block.scalar
        def _(eng):
            S.play(eng, "scalar", sems)

        @block.gpsimd
        def _(eng):
            S.play(eng, "gpsimd", sems)
            for i in range(4):
                eng.wait_ge(sems["dG" + str(i)], S.c("dG" + str(i)))

    return nc


_NC = None


def _get_nc():
    global _NC
    if _NC is None:
        _NC = build_program()
    return _NC


def kernel(L0_attn_hidden_states, L0_attn_Wq, L0_attn_Wk, L0_attn_Wv,
           L0_attn_conv_q, L0_attn_conv_k, L0_attn_conv_v, L0_attn_Wb,
           L0_attn_Wdec, L0_attn_Wmix, L0_attn_conv_mix, L0_attn_mix_bias,
           L0_attn_rms_w, L0_attn_Wo, L0_ffn_weight, L0_ffn_bias):
    x = np.ascontiguousarray(np.asarray(L0_attn_hidden_states, np.float32))
    Wq = np.asarray(L0_attn_Wq, np.float32)
    Wk = np.asarray(L0_attn_Wk, np.float32)
    Wv = np.asarray(L0_attn_Wv, np.float32)
    cq = np.asarray(L0_attn_conv_q, np.float32)
    ck = np.asarray(L0_attn_conv_k, np.float32)
    cv = np.asarray(L0_attn_conv_v, np.float32)
    Wb = np.asarray(L0_attn_Wb, np.float32)
    Wd = np.asarray(L0_attn_Wdec, np.float32)
    Wm = np.asarray(L0_attn_Wmix, np.float32)
    cmix = np.asarray(L0_attn_conv_mix, np.float32)
    mbias = np.asarray(L0_attn_mix_bias, np.float32)
    rmsw = np.asarray(L0_attn_rms_w, np.float32)
    Wo = np.asarray(L0_attn_Wo, np.float32)
    ffw = np.asarray(L0_ffn_weight, np.float32)
    ffb = np.asarray(L0_ffn_bias, np.float32)

    def blk3(a, inner):
        return np.ascontiguousarray(a.reshape(8, 128, inner).transpose(1, 0, 2))

    r = np.arange(128)
    bi, bj = r[:, None] // CH, r[None, :] // CH
    mLTh = ((bi == bj) & (r[None, :] > r[:, None])).astype(np.float32)
    mLh = ((bi == bj) & (r[None, :] < r[:, None])).astype(np.float32)
    mGLh = (bi > bj).astype(np.float32)
    mGUh = (bj > bi).astype(np.float32)
    mATh = ((bj > bi) | ((bi == bj) & (r[None, :] >= r[:, None]))
            ).astype(np.float32)

    def _mk_mask(c):
        hp = c % 2
        m = np.zeros((1, 16), np.float32)
        m[0, 2] = 1.0 if hp == 0 else 0.0
        m[0, 3] = 1.0 - m[0, 2]
        m[0, 4 + (c ^ 1)] = 1.0
        return m

    in_maps = []
    for c in range(8):
        b, hp = c // 2, c % 2
        hs = slice(hp * 512, (hp + 1) * 512)
        wgm = np.stack([Wb[:, 2 * hp], Wb[:, 2 * hp + 1],
                        Wd[:, 2 * hp], Wd[:, 2 * hp + 1],
                        Wm[:, 2 * hp], Wm[:, 2 * hp + 1]], axis=1)
        cvm = np.concatenate([cq[hs], ck[hs], cv[hs]], 0)
        in_maps.append({
            "x": np.ascontiguousarray(x[b]),
            "xh": np.ascontiguousarray(x[b, hp * HALF:(hp + 1) * HALF]),
            "wq": blk3(np.ascontiguousarray(Wq[:, hs]), 512),
            "wk": blk3(np.ascontiguousarray(Wk[:, hs]), 512),
            "wv": blk3(np.ascontiguousarray(Wv[:, hs]), 512),
            "wg": blk3(np.ascontiguousarray(wgm), 6),
            "cvw": np.ascontiguousarray(
                cvm.reshape(12, 128, 4).transpose(1, 0, 2)),
            "cmx": np.ascontiguousarray(cmix[2 * hp:2 * hp + 2]),
            "mb2": np.ascontiguousarray(mbias[2 * hp:2 * hp + 2, None]),
            "rw": np.ascontiguousarray(np.tile(rmsw[None, :], (128, 1))),
            "idm": np.eye(128, dtype=np.float32),
            "mLT": mLTh, "mL": mLh, "mGL": mGLh, "mGU": mGUh, "mAT": mATh,
            "msk": _mk_mask(c),
            "wo": blk3(np.ascontiguousarray(Wo), 1024),
            "fw": blk3(np.ascontiguousarray(ffw.T), 1024),
            "fb": np.ascontiguousarray(ffb.reshape(8, 128).T),
        })

    nc = _get_nc()
    res = run_bass_kernel_spmd(nc, in_maps, core_ids=list(range(8)))
    out = np.empty((B, L, D), np.float32)
    for c in range(8):
        b, hp = c // 2, c % 2
        out[b, hp * HALF:(hp + 1) * HALF, :] = res.results[c]["out"].T
    return out


# version 16
# speedup vs baseline: 1.0219x; 1.0219x over previous
import numpy as np
import concourse.bass as bass
import concourse.mybir as mybir
from concourse.bass_utils import run_bass_kernel_spmd

F32 = mybir.dt.float32
BF16 = mybir.dt.bfloat16
ALU = mybir.AluOpType
AFT = mybir.ActivationFunctionType
AXL = mybir.AxisListType

B, L, D = 4, 8192, 1024
H2, DK, CH = 2, 256, 32
MAC, NSUB = 256, 2
NMAC = L // MAC
HALF = L // 2
MAC2, NSUB2 = 512, 4
NOM = HALF // MAC2
EPS_LN, EPS_L2, EPS_RMS = 1e-5, 1e-6, 1e-5
DBG_NMAC = None
DBG_XNT = True
DBG_DUMP = False
DBG_P2 = None
DBG_LEVEL = 9
DBG_PHASE2 = True
REC127 = 1.0 / 127.0

def CP(e, d, s):
    return e.tensor_copy(d, s) if hasattr(e, "tensor_copy") else e.copy(d, s)


ESEM = {"sync": "dX", "tensor": "T", "vector": "V", "scalar": "C",
        "gpsimd": "G"}


class Sched:
    def __init__(self):
        self.items = []
        self.buf = None
        self.tag = "pre"
        self.meta = []
        self.cnt = {"T": 0, "V": 0, "C": 0, "G": 0}
        for q in ("dX", "dG"):
            for i in range(4):
                self.cnt[q + str(i)] = 0
        self.ring = {"dX": 0, "dG": 0}
        self.wr = {}
        self.rd = {}
        self.seen = {}

    def _edge_waits(self, reads, writes):
        w = {}
        for k in reads:
            if k in self.wr:
                s, c = self.wr[k]
                if w.get(s, 0) < c:
                    w[s] = c
        for k in writes:
            if k in self.wr:
                s, c = self.wr[k]
                if w.get(s, 0) < c:
                    w[s] = c
            for s, c in self.rd.get(k, {}).items():
                if w.get(s, 0) < c:
                    w[s] = c
        return w

    def begin(self):
        self.buf = []

    def take(self):
        b = self.buf
        self.buf = None
        return b

    def emit(self, lst):
        for a in lst:
            if len(a) > 7:
                self.tag = a[7]
                self.mmw = a[8]
            self.op(*a[:7])

    MERGE_W = {"back-B": 2.0}

    def merge(self, A, B):
        out = []
        if not A:
            out = list(B)
        else:
            wts = [self.MERGE_W.get(a[7] if len(a) > 7 else "", 1.0)
                   for a in A]
            tot = sum(wts)
            acc, bi = 0.0, 0
            for a, wt in zip(A, wts):
                out.append(a)
                acc += len(B) * wt / tot
                while bi < len(B) and acc >= 1.0:
                    out.append(B[bi])
                    bi += 1
                    acc -= 1.0
            out.extend(B[bi:])
        for a in out:
            if len(a) > 7:
                self.tag = a[7]
                self.mmw = a[8]
            self.op(*a[:7])

    def op(self, eng, fn, reads=(), writes=(), sem=None, incn=1, extra=()):
        if self.buf is not None:
            self.buf.append((eng, fn, tuple(reads), tuple(writes), sem,
                             incn, tuple(extra), self.tag,
                             getattr(self, "mmw", 0)))
            self.mmw = 0
            return
        self.meta.append((eng, self.tag, getattr(self, "mmw", 0)))
        self.mmw = 0
        w = self._edge_waits(reads, writes)
        for s, c in extra:
            if w.get(s, 0) < c:
                w[s] = c
        sem = sem or ESEM[eng]
        if sem in ("dX", "dG"):
            slot = self.ring[sem]
            self.ring[sem] = (slot + 1) % 4
            sem = sem + str(slot)
            if w.get(sem, 0) < self.cnt[sem]:
                w[sem] = self.cnt[sem]
        self.cnt[sem] += incn
        cnt = self.cnt[sem]
        seen = self.seen.setdefault(eng, {})
        waits = []
        for s, c in w.items():
            if c > 0 and seen.get(s, -1) < c:
                waits.append((s, c))
                seen[s] = c
        self.items.append((eng, fn, waits, [(sem, incn)]))
        for k in writes:
            self.wr[k] = (sem, cnt)
            self.rd[k] = {}
        for k in reads:
            d = self.rd.setdefault(k, {})
            if d.get(sem, 0) < cnt:
                d[sem] = cnt

    def c(self, sem):
        return self.cnt[sem]

    def play(self, eng_obj, name, sems):
        for eng, fn, waits, incs in self.items:
            if eng != name:
                continue
            for s, v in waits:
                eng_obj.wait_ge(sems[s], v)
            inst = fn(eng_obj)
            for s, n in incs:
                inst.then_inc(sems[s], n)


def build_program():
    nc = bass.Bass("TRN2", target_bir_lowering=False)
    S = Sched()

    DI = lambda n, sh: nc.dram_tensor(n, sh, F32, kind="ExternalInput").ap()
    d_x = DI("x", [L, D]); d_xh = DI("xh", [HALF, D])
    d_wq = DI("wq", [128, 8, 512]); d_wk = DI("wk", [128, 8, 512])
    d_wv = DI("wv", [128, 8, 512]); d_wg = DI("wg", [128, 8, 6])
    d_cvw = DI("cvw", [128, 12, 4]); d_cmx = DI("cmx", [2, 4])
    d_mb2 = DI("mb2", [2, 1]); d_rw = DI("rw", [128, 256])
    d_idm = DI("idm", [128, 128]); d_mLT = DI("mLT", [128, 128])
    d_mL = DI("mL", [128, 128]); d_mGL = DI("mGL", [128, 128])
    d_mGU = DI("mGU", [128, 128]); d_mAT = DI("mAT", [128, 128])
    d_wo = DI("wo", [128, 8, 1024]); d_fw = DI("fw", [128, 8, 1024])
    d_fb = DI("fb", [128, 8])
    d_out = nc.dram_tensor("out", [1024, HALF], F32, kind="ExternalOutput").ap()
    d_msk = DI("msk", [1, 16])
    d_oall = nc.dram_tensor("oall", [2, HALF, 512], F32).ap()
    d_agi = nc.dram_tensor("agi", [HALF, 512], F32).ap()
    d_ago = nc.dram_tensor("ago", [8, HALF, 512], F32).ap()
    d_mxi = nc.dram_tensor("mxi", [1, 16], F32).ap()
    d_mxo = nc.dram_tensor("mxo", [1, 16], F32).ap()

    OFF = [18432]

    def SB(name, shape, dt=F32):
        nb = int(np.prod(shape[1:])) * mybir.dt.size(dt)
        nb = (nb + 31) // 32 * 32
        h = nc.alloc_sbuf_tensor_at(name, shape, dt, offset=OFF[0])
        OFF[0] += nb
        return h.ap()

    ident = SB("ident", [128, 128]); ones1 = SB("ones1", [1, 128])
    mLT = SB("mLT", [128, 128]); mL = SB("mL", [128, 128])
    mGL = SB("mGL", [128, 128]); mGU = SB("mGU", [128, 128])
    mAT = SB("mAT", [128, 128]); rmswt = SB("rmswt", [128, 256])
    cvw = SB("cvw", [128, 12, 4]); cmx = SB("cmx", [2, 4]); mb2 = SB("mb2", [2, 1])
    cols = SB("cols", [128, 16])
    colw = SB("colw", [128, 16])
    colsP = SB("colsP", [128, 2, NSUB, 2])
    scal = SB("scal", [1, 16]); bcc = SB("bcc", [128, 8])
    ecar = SB("ecar", [128, 4]); S0sb = SB("S0sb", [128, 2, 2, 256])
    rmx = SB("rmx", [128, 2])
    epsc = SB("epsc", [128, 2])
    MARK = OFF[0]

    wqs = SB("wqs", [128, 8, 512]); wks = SB("wks", [128, 8, 512])
    wvs = SB("wvs", [128, 8, 512]); wgs = SB("wgs", [128, 8, 6])
    xt = SB("xt", [128, 2, 1024]); xnb = SB("xnb", [128, 1, 1024])
    scr = SB("scr", [128, 256]); xnT = SB("xnT", [128, 2, 8, MAC])
    proj = SB("proj", [128, 2, 12, MAC + 3]); pcv = SB("pcv", [128, 2, 12, MAC])
    gtm = SB("gtm", [128, NSUB, 6]); grow = SB("grow", [2, MAC + 3])
    gbc = SB("gbc", [128, MAC]); geb = SB("geb", [128, MAC])
    gam2 = SB("gam2", [1, 2, 2 * MAC])
    betac = SB("betac", [128, 2, NSUB, 2]); mixc = SB("mixc", [128, 2, NSUB, 2])
    kTM = SB("kTM", [128, 2, NSUB, 256]); KbTM = SB("KbTM", [128, 2, NSUB, 256])
    VbTM = SB("VbTM", [128, 2, NSUB, 256])
    qsc = scr
    khT = SB("khT", [128, 2, 2, MAC]); kbT = SB("kbT", [128, 2, 2, MAC])
    LTs = SB("LTs", [128, 2, 128]); Ls = SB("Ls", [128, 2, 128])
    LT2 = SB("LT2", [128, 2, 128]); L2 = SB("L2", [128, 2, 128])
    LT4 = SB("LT4", [128, 2, 128]); L4 = SB("L4", [128, 2, 128])
    LT8 = SB("LT8", [128, 2, 128]); L8 = SB("L8", [128, 2, 128])
    LT16 = SB("LT16", [128, 2, 128]); Rb = SB("Rb", [128, 2, 2, 128])
    MTs = SB("MTs", [128, 2, NSUB, 128])
    WpTn = SB("WpTn", [128, 2, NSUB, 2, 128])
    Gns = SB("Gns", [128, 2, NSUB, 128]); GTn = SB("GTn", [128, 2, NSUB, 128])
    GT2s = SB("GT2s", [128, 2, NSUB, 128]); ATm = SB("ATm", [128, 2, 128])
    byu = SB("byu", [128, 2, NSUB, 256])
    ueb = SB("ueb", [128, MAC]); esc = SB("esc", [128, MAC])
    emaTM = SB("emaTM", [128, 2, NSUB, 256])
    occ = SB("occ", [128, 2, 256]); oft = SB("oft", [128, 2, 256])
    assert OFF[0] <= 229344, f"phase1 sbuf overflow {OFF[0]}"

    OFF[0] = MARK
    fbuf = SB("fbuf", [128, 8, 1024]); qwb = SB("qwb", [128, 8, 1024], BF16)
    wob = SB("wob", [128, 8, 1024]); xt2 = SB("xt2", [128, 2, 1024])
    xn2 = SB("xn2", [128, 1024]); z0b = SB("z0b", [128, 1024])
    zwb = SB("zwb", [128, 1024]); scr2 = SB("scr2", [128, 1024])
    qib = SB("qib", [128, 1024]); qiT = SB("qiT", [128, 8, 512], BF16)
    otm = SB("otm", [128, 2, 512]); oT = SB("oT", [128, 8, 512])
    gbuf = SB("gbuf", [128, 8, 512]); xTb = SB("xTb", [128, 8, 512])
    fbc = SB("fbc", [128, 8]); outb = SB("outb", [128, 2, 512])
    mrw = SB("mrw", [1, 16]); mcl = SB("mcl", [128, 16])
    tcand = SB("tcand", [128, 2, 512]); town = SB("town", [128, 2, 512])
    tpar = SB("tpar", [128, 512])
    xTb2 = nc.alloc_sbuf_tensor_at("xTb2", [128, 8, 512], F32,
                                   offset=MARK).ap()
    qiT2 = nc.alloc_sbuf_tensor_at("qiT2", [128, 8, 512], BF16,
                                   offset=MARK + 16384).ap()
    assert OFF[0] <= 229344, f"phase2 sbuf overflow {OFF[0]}"

    PSn = lambda n: nc.alloc_psum_tensor(n, [128, 512], F32).ap()
    pS0, pS1 = PSn("pS0"), PSn("pS1")
    pBa, pBb = PSn("pBa"), PSn("pBb")
    pCa, pCb = PSn("pCa"), PSn("pCb")
    pOa, pOb = PSn("pOa"), PSn("pOb")
    BK = {"S0": pS0, "S1": pS1, "Ba": pBa, "Bb": pBb,
          "Ca": pCa, "Cb": pCb, "Oa": pOa, "Ob": pOb}
    import itertools as _it
    TROT = _it.cycle(["Oa", "Ob", "Ba", "Bb"])

    RGALL = (0, 1, 2, 3)

    def mm(bk, out, lhsT, rhs, start, stop, reads=(), extra=(), rg=RGALL):
        S.mmw = int(out.shape[-1])
        S.op("tensor",
             lambda e, o=out, l=lhsT, r=rhs, s1=start, s2=stop:
             e.matmul(o, l, r, start=s1, stop=s2),
             reads=list(reads), writes=["bk:" + bk], extra=extra)

    def tr(bk, out, in_, reads=(), extra=(), rg=RGALL):
        idp = ident[0:in_.shape[0], 0:in_.shape[0]]
        S.mmw = -int(in_.shape[-1])
        S.op("tensor", lambda e, o=out, i=in_, idm=idp: e.transpose(o, i, idm),
             reads=list(reads) + ["ident"],
             writes=["bk:" + bk], extra=extra)

    def ev(eng, bk, fn, reads=(), writes=(), extra=(), rg=RGALL):
        S.op(eng, fn, reads=list(reads) + ["bk:" + bk],
             writes=list(writes), extra=extra)

    def vo(fn, reads=(), writes=(), extra=()):
        S.op("vector", fn, reads=reads, writes=writes, extra=extra)

    def co(fn, reads=(), writes=(), extra=()):
        S.op("scalar", fn, reads=reads, writes=writes, extra=extra)

    def dma(eng, out, in_, reads=(), writes=(), extra=()):
        sem = "dG" if eng == "gpsimd" else "dX"
        S.op(eng, lambda e, o=out, i=in_: e.dma_start(out=o, in_=i),
             reads=reads, writes=writes, sem=sem, incn=16, extra=extra)

    for key, out, src in [("ident", ident[:], d_idm), ("mLT", mLT[:], d_mLT),
                          ("mL", mL[:], d_mL), ("mGL", mGL[:], d_mGL),
                          ("mGU", mGU[:], d_mGU), ("mAT", mAT[:], d_mAT),
                          ("rmswt", rmswt[:], d_rw), ("cvw", cvw[:], d_cvw),
                          ("cmx", cmx[:], d_cmx), ("mb2", mb2[:], d_mb2),
                          ("wqs", wqs[:], d_wq), ("wks", wks[:], d_wk),
                          ("wvs", wvs[:], d_wv), ("wgs", wgs[:], d_wg)]:
        dma("sync", out, src, writes=[key])
    vo(lambda e: e.memset(ones1[:], 1.0), writes=["ones1"])
    vo(lambda e: e.memset(epsc[:, 0:1], EPS_LN), writes=["epsc"])
    vo(lambda e: e.memset(epsc[:, 1:2], EPS_L2), writes=["epsc"])
    vo(lambda e: e.memset(rmx[:], 0.0), writes=["rmx"])
    vo(lambda e: e.memset(ecar[:], 0.0), writes=["ecar"])
    vo(lambda e: e.memset(S0sb[:].rearrange("p a b c -> p (a b c)"), 0.0),
       writes=["S0sb:0", "S0sb:1"])
    vo(lambda e: e.memset(proj[:].rearrange("p a b c -> p (a b c)"), 0.0),
       writes=[f"proj{fam}:{b}:{j}" for fam in ("h", "d")
               for b in range(2) for j in range(12)])
    vo(lambda e: e.memset(grow[:], 0.0), writes=["grow"])

    FSLOT = _it.cycle(["Oa", "Ob"])
    btr_state = [0]
    btr_cur = {"Ba": 0, "Bb": 0}

    def btr(n):
        bank = ("Ba", "Bb")[btr_state[0] % 2]
        btr_state[0] += 1
        if n == 4:
            return bank, 0
        r0 = btr_cur[bank]
        btr_cur[bank] = 2 - r0
        return bank, r0

    def front(m):
        t0 = m * MAC
        bf = m % 2
        S.tag = "front-ln"
        for sub in range(NSUB):
            gi = m * NSUB + sub
            slx = xt[:, gi % 2, :]
            kx = f"xt:{gi % 2}"
            dma("sync", slx, d_x[t0 + sub * 128:t0 + sub * 128 + 128, :],
                writes=[kx])
            c4 = sub * 4
            vo(lambda e, s=slx, c=c4: e.reduce_sum(cols[:, c:c + 1], s,
                                                   axis=AXL.X),
               reads=[kx], writes=[f"cols:{c4}"])
            xs = xnb[:, 0, :]
            kxn = "xn"
            co(lambda e, c=c4: e.activation(cols[:, c + 1:c + 2],
                                            cols[:, c:c + 1], AFT.Copy,
                                            scale=-1.0 / D),
               reads=[f"cols:{c4}"], writes=[f"cols:{c4 + 1}"])
            co(lambda e, s=slx, x_=xs, c=c4: e.activation(
                x_, s, AFT.Square, bias=cols[:, c + 1:c + 2],
                accum_out=cols[:, c + 2:c + 3]),
               reads=[kx, f"cols:{c4 + 1}"], writes=[kxn, f"cols:{c4 + 2}"])
            co(lambda e, c=c4: e.activation(cols[:, c + 3:c + 4],
                                            cols[:, c + 2:c + 3], AFT.Sqrt,
                                            scale=1.0 / D, bias=epsc[:, 0:1]),
               reads=[f"cols:{c4 + 2}", "epsc"], writes=[f"cols:{c4 + 3}"])
            vo(lambda e, c=c4: e.reciprocal(cols[:, c + 3:c + 4],
                                            cols[:, c + 3:c + 4]),
               reads=[f"cols:{c4 + 3}"], writes=[f"cols:{c4 + 3}"])
            vo(lambda e, s=slx, x_=xs, c=c4: e.tensor_scalar(
                x_, s, cols[:, c + 1:c + 2], cols[:, c + 3:c + 4],
                ALU.add, ALU.mult),
               reads=[kx, f"cols:{c4 + 1}", f"cols:{c4 + 3}"], writes=[kxn])
            vo(lambda e, x_=xs: e.reduce_max(rmx[:, 1:2], x_, axis=AXL.X,
                                             apply_absolute_value=True),
               reads=[kxn], writes=["rmx1"])
            vo(lambda e: e.tensor_max(rmx[:, 0:1], rmx[:, 0:1], rmx[:, 1:2]),
               reads=["rmx1"], writes=["rmx"])
            S.tag = "front-xnT"
            for half in (range(2) if DBG_XNT else []):
                bk = next(FSLOT)
                for j in range(4):
                    db = half * 4 + j
                    tr(bk, BK[bk][:, j * 128:(j + 1) * 128],
                       xs[:, db * 128:(db + 1) * 128], reads=[kxn])
                dst = xnT[:, bf, half * 4:half * 4 + 4,
                          sub * 128:(sub + 1) * 128]
                ev("scalar", bk,
                   lambda e, d=dst, b=BK[bk]: CP(
                       e, d, b[:].rearrange("p (j t) -> p j t", j=4)),
                   writes=[f"xnT:{bf}:{half * 4 + j}:{sub}"
                           for j in range(4)])

        if DBG_LEVEL < 2:
            return

        S.tag = "front-proj"
        for dk in range(12):
            bk = next(FSLOT)
            wsrc, wkey = ((wqs, "wqs"), (wks, "wks"), (wvs, "wvs"))[dk // 4]
            dkl = (dk % 4) * 128
            for db in range(8):
                mm(bk, BK[bk][:, 0:MAC], wsrc[:, db, dkl:dkl + 128],
                   xnT[:, bf, db, :],
                   start=(db == 0), stop=(db == 7),
                   reads=[wkey] + [f"xnT:{bf}:{db}:{s}" for s in range(NSUB)])
            dst = proj[:, bf, dk, 3:MAC + 3]
            ev("scalar", bk,
               lambda e, d=dst, b=BK[bk]: CP(e, d, b[:, 0:MAC]),
               writes=[f"projd:{bf}:{dk}"])

        if DBG_LEVEL < 3:
            return
        S.tag = "front-gates"
        for sub in range(NSUB):
            off = sub * 8
            for db in range(8):
                mm("Oa", pOa[:, off:off + 6],
                   xnT[:, bf, db, sub * 128:(sub + 1) * 128], wgs[:, db, :],
                   start=(db == 0), stop=(db == 7),
                   reads=["wgs", f"xnT:{bf}:{db}:{sub}"])
            ev("scalar", "Oa",
               lambda e, s=sub, o=off: e.copy(gtm[:, s, :], pOa[:, o:o + 6]),
               writes=[f"gtm:{sub}"])
            co(lambda e, s=sub: e.activation(betac[:, bf, s, :],
                                             gtm[:, s, 0:2], AFT.Sigmoid),
               reads=[f"gtm:{sub}"], writes=[f"betac:{bf}:{sub}"])
        for sub in range(NSUB):
            tr("Ob", pOb[0:2, sub * 128:(sub + 1) * 128],
               gtm[:, sub, 4:6], reads=[f"gtm:{sub}"])
            tr("Ob", pOb[0:1, 256 + sub * 128:256 + (sub + 1) * 128],
               gtm[:, sub, 2:3], reads=[f"gtm:{sub}"])
        ev("scalar", "Ob", lambda e: e.copy(grow[:, 3:MAC + 3],
                                            pOb[0:2, 0:MAC]),
           reads=["grow"], writes=["growd"])
        ev("scalar", "Ob", lambda e: e.activation(
            gam2[0:1, bf, 0:MAC], pOb[0:1, 256:256 + MAC], AFT.Sigmoid),
           writes=[f"gam2:{bf}"])
        for sub in range(NSUB):
            tr("Oa", pOa[0:1, sub * 128:(sub + 1) * 128],
               gtm[:, sub, 3:4], reads=[f"gtm:{sub}"])
        ev("scalar", "Oa", lambda e: e.activation(
            gam2[0:1, bf, MAC:2 * MAC], pOa[0:1, 0:MAC], AFT.Sigmoid),
           writes=[f"gam2:{bf}"])
        vo(lambda e: e.tensor_scalar(gbc[0:2, 0:MAC], grow[0:2, 0:MAC],
                                     cmx[:, 0:1], None, ALU.mult),
           reads=["growd", "grow", "cmx"], writes=["gbc"])
        for k in range(1, 4):
            vo(lambda e, kk=k: e.scalar_tensor_tensor(
                gbc[0:2, 0:MAC], grow[0:2, kk:kk + MAC], cmx[:, kk:kk + 1],
                gbc[0:2, 0:MAC], ALU.mult, ALU.add),
               reads=["growd", "grow", "cmx", "gbc"], writes=["gbc"])
        co(lambda e: e.activation(gbc[0:2, 0:MAC], gbc[0:2, 0:MAC], AFT.Silu),
           reads=["gbc"], writes=["gbc"])
        co(lambda e: e.activation(gbc[0:2, 0:MAC], gbc[0:2, 0:MAC],
                                  AFT.Sigmoid, bias=mb2[:]),
           reads=["gbc", "mb2"], writes=["gbc"])
        co(lambda e: e.copy(grow[:, 0:3], grow[:, MAC:MAC + 3]),
           reads=["growd"], writes=["grow"])
        for sub in range(NSUB):
            tr("Ob", pOb[0:128, sub * 128:sub * 128 + 2],
               gbc[0:2, sub * 128:(sub + 1) * 128], reads=["gbc"])
        for sub in range(NSUB):
            ev("scalar", "Ob",
               lambda e, s=sub: e.copy(mixc[:, bf, s, :],
                                       pOb[0:128, s * 128:s * 128 + 2]),
               writes=[f"mixc:{bf}:{sub}"])

        if DBG_LEVEL < 4:
            return
        S.tag = "front-conv"
        for j in range(12):
            kj = [f"projd:{bf}:{j}", f"projh:{bf}:{j}", "cvw"]
            vo(lambda e, jj=j: e.tensor_scalar(
                pcv[:, bf, jj, :], proj[:, bf, jj, 0:MAC], cvw[:, jj, 0:1],
                None, ALU.mult), reads=kj, writes=[f"pcv:{bf}:{j}"])
            for k in range(1, 4):
                vo(lambda e, jj=j, kk=k: e.scalar_tensor_tensor(
                    pcv[:, bf, jj, :], proj[:, bf, jj, kk:kk + MAC],
                    cvw[:, jj, kk:kk + 1], pcv[:, bf, jj, :],
                    ALU.mult, ALU.add),
                   reads=kj + [f"pcv:{bf}:{j}"], writes=[f"pcv:{bf}:{j}"])
            vo(lambda e, jj=j: e.tensor_copy(proj[:, 1 - bf, jj, 0:3],
                                             proj[:, bf, jj, MAC:MAC + 3]),
               reads=[f"projd:{bf}:{j}"], writes=[f"projh:{1 - bf}:{j}"])
            for si in range(2 if j < 8 else 1):
                co(lambda e, jj=j: e.activation(pcv[:, bf, jj, :],
                                                pcv[:, bf, jj, :], AFT.Silu),
                   reads=[f"pcv:{bf}:{j}"], writes=[f"pcv:{bf}:{j}"])

    def back(m):
        bf = m % 2
        t0 = m * MAC
        if DBG_LEVEL < 5:
            return
        if DBG_LEVEL < 6:
            return
        S.tag = "back-tm"
        for h in range(H2):
            for sub in range(NSUB):
                bk, r0 = btr(4)
                for idx, j in enumerate([h * 2, h * 2 + 1,
                                         4 + h * 2, 4 + h * 2 + 1]):
                    tr(bk, BK[bk][:, idx * 128:(idx + 1) * 128],
                       pcv[:, bf, j, sub * 128:(sub + 1) * 128],
                       reads=[f"pcv:{bf}:{j}"], rg=(idx,))
                ev("vector", bk, lambda e, b=BK[bk]: e.tensor_copy(
                    qsc[:], b[:, 0:256]), writes=["scr"], rg=(0, 1))
                ev("vector", bk, lambda e, b=BK[bk], hh=h, ss=sub:
                   e.tensor_copy(kTM[:, hh, ss, :], b[:, 256:512]),
                   writes=[f"kTM:{h}:{sub}"], rg=(2, 3))
                bk, r0 = btr(2)
                for idx, j in enumerate([8 + h * 2, 8 + h * 2 + 1]):
                    tr(bk, BK[bk][:, (r0 + idx) * 128:(r0 + idx + 1) * 128],
                       pcv[:, bf, j, sub * 128:(sub + 1) * 128],
                       reads=[f"pcv:{bf}:{j}"], rg=(r0 + idx,))
                ev("vector", bk, lambda e, b=BK[bk], hh=h, ss=sub, r=r0:
                   e.tensor_copy(VbTM[:, hh, ss, :],
                                 b[:, r * 128:(r + 2) * 128]),
                   writes=[f"VbTM:{h}:{sub}"], rg=(r0, r0 + 1))
                co(lambda e: e.activation(scr[:, 0:256], qsc[:], AFT.Square,
                                          accum_out=colw[:, 0:1]),
                   reads=["scr"], writes=["scr", "colw0"])
                co(lambda e: e.activation(colw[:, 1:2], colw[:, 0:1],
                                          AFT.Sqrt, bias=epsc[:, 1:2]),
                   reads=["colw0", "epsc"], writes=["colw1"])
                vo(lambda e: e.reciprocal(colw[:, 1:2], colw[:, 1:2]),
                   reads=["colw1"], writes=["colw1"])
                co(lambda e, hh=h, ss=sub: e.activation(
                    scr[:, 0:256], kTM[:, hh, ss, :], AFT.Square,
                    accum_out=colw[:, 2:3]),
                   reads=[f"kTM:{h}:{sub}"], writes=["scr", "colw2"])
                co(lambda e: e.activation(colw[:, 3:4], colw[:, 2:3],
                                          AFT.Sqrt, bias=epsc[:, 1:2]),
                   reads=["colw2", "epsc"], writes=["colw3"])
                vo(lambda e: e.reciprocal(colw[:, 3:4], colw[:, 3:4]),
                   reads=["colw3"], writes=["colw3"])
                vo(lambda e, hh=h, ss=sub: e.tensor_mul(
                    colw[:, 4:5], colw[:, 3:4], betac[:, bf, ss, hh:hh + 1]),
                   reads=["colw3", f"betac:{bf}:{sub}"], writes=["colw4"])
                vo(lambda e, hh=h, ss=sub: e.tensor_scalar(
                    KbTM[:, hh, ss, :], kTM[:, hh, ss, :], colw[:, 4:5],
                    None, ALU.mult),
                   reads=[f"kTM:{h}:{sub}", "colw4"],
                   writes=[f"KbTM:{h}:{sub}"])
                vo(lambda e, hh=h, ss=sub: e.tensor_scalar(
                    kTM[:, hh, ss, :], kTM[:, hh, ss, :], colw[:, 3:4],
                    None, ALU.mult),
                   reads=[f"kTM:{h}:{sub}", "colw3"],
                   writes=[f"kTM:{h}:{sub}"])
                vo(lambda e, hh=h, ss=sub: e.tensor_scalar(
                    VbTM[:, hh, ss, :], VbTM[:, hh, ss, :],
                    betac[:, bf, ss, hh:hh + 1], None, ALU.mult),
                   reads=[f"VbTM:{h}:{sub}", f"betac:{bf}:{sub}"],
                   writes=[f"VbTM:{h}:{sub}"])
                vo(lambda e, hh=h, ss=sub: e.tensor_scalar(
                    colw[:, 5:6], mixc[:, bf, ss, hh:hh + 1], -1.0, 1.0,
                    ALU.mult, ALU.add),
                   reads=[f"mixc:{bf}:{sub}"], writes=["colw5"])
                vo(lambda e: e.tensor_mul(colw[:, 6:7], colw[:, 1:2],
                                          colw[:, 5:6]),
                   reads=["colw1", "colw5"], writes=["colw6"])
                vo(lambda e, hh=h, ss=sub: e.tensor_copy(
                    colsP[:, hh, ss, 0:1], colw[:, 6:7]),
                   reads=["colw6"], writes=[f"colsP:{h}:{sub}"])
                vo(lambda e, hh=h, ss=sub: e.tensor_copy(
                    colsP[:, hh, ss, 1:2], mixc[:, bf, ss, hh:hh + 1]),
                   reads=[f"mixc:{bf}:{sub}"], writes=[f"colsP:{h}:{sub}"])
                bk, r0 = btr(4)
                for idx, srcS in enumerate([
                        kTM[:, h, sub, 0:128], kTM[:, h, sub, 128:256],
                        KbTM[:, h, sub, 0:128], KbTM[:, h, sub, 128:256]]):
                    tr(bk, BK[bk][:, idx * 128:(idx + 1) * 128], srcS,
                       reads=[f"kTM:{h}:{sub}" if idx < 2 else
                              f"KbTM:{h}:{sub}"], rg=(idx,))
                ev("vector", bk, lambda e, hh=h, ss=sub, b=BK[bk]: CP(
                    e, khT[:, hh, :, ss * 128:(ss + 1) * 128],
                    b[:, 0:256].rearrange("p (j t) -> p j t", j=2)),
                   writes=[f"khT:{h}:{sub}"], rg=(0, 1))
                ev("vector", bk, lambda e, hh=h, ss=sub, b=BK[bk]: CP(
                    e, kbT[:, hh, :, ss * 128:(ss + 1) * 128],
                    b[:, 256:512].rearrange("p (j t) -> p j t", j=2)),
                   writes=[f"kbT:{h}:{sub}"], rg=(2, 3))

        if DBG_LEVEL < 7:
            return
        S.tag = "back-A"
        for sc in range(NSUB):
            for h in range(H2):
                par = h
                bk = "Ca" if par == 0 else "Cb"
                b = BK[bk]
                kh = [f"khT:{h}:{s}" for s in range(NSUB)]
                kb_ = [f"kbT:{h}:{s}" for s in range(NSUB)]
                for blk in range(2):
                    mm(bk, b[:, 0:128],
                       khT[:, h, blk, sc * 128:(sc + 1) * 128],
                       kbT[:, h, blk, sc * 128:(sc + 1) * 128],
                       start=(blk == 0), stop=(blk == 1), reads=kh + kb_,
                       rg=(0,))
                for blk in range(2):
                    mm(bk, b[:, 128:256],
                       kbT[:, h, blk, sc * 128:(sc + 1) * 128],
                       khT[:, h, blk, sc * 128:(sc + 1) * 128],
                       start=(blk == 0), stop=(blk == 1), reads=kh + kb_,
                       rg=(1,))
                ev("vector", bk, lambda e, p=par, b_=b: e.tensor_mul(
                    LTs[:, p, :], b_[:, 0:128], mLT[:]),
                   reads=["mLT"], writes=[f"LTs:{par}"], rg=(0,))
                ev("vector", bk, lambda e, p=par, b_=b: e.tensor_mul(
                    Ls[:, p, :], b_[:, 128:256], mL[:]),
                   reads=["mL"], writes=[f"Ls:{par}"], rg=(1,))
                sq = [(Ls, "Ls", LTs, "LTs", LT2, "LT2"),
                      (LTs, "LTs", Ls, "Ls", L2, "L2"),
                      (L2, "L2", LT2, "LT2", LT4, "LT4"),
                      (LT2, "LT2", L2, "L2", L4, "L4"),
                      (L4, "L4", LT4, "LT4", LT8, "LT8"),
                      (LT4, "LT4", L4, "L4", L8, "L8"),
                      (L8, "L8", LT8, "LT8", LT16, "LT16")]
                for a, ka, bb_, kbb, dstT, kd in sq:
                    mm(bk, b[:, 256:384], a[:, par, :], bb_[:, par, :],
                       start=True, stop=True,
                       reads=[f"{ka}:{par}", f"{kbb}:{par}"], rg=(2,))
                    ev("vector", bk, lambda e, d=dstT, p=par, b_=b:
                       e.tensor_copy(d[:, p, :], b_[:, 256:384]),
                       writes=[f"{kd}:{par}"], rg=(2,))
                vo(lambda e, p=par: e.tensor_add(Rb[:, p, 0, :],
                                                 LT16[:, p, :], ident[:]),
                   reads=[f"LT16:{par}", "ident"], writes=[f"Rb:{par}:0"])
                cur = 0
                for pw, kpw in [(L8, "L8"), (L4, "L4"), (L2, "L2")]:
                    mm(bk, b[:, 384:512], pw[:, par, :], Rb[:, par, cur, :],
                       start=True, stop=True,
                       reads=[f"{kpw}:{par}", f"Rb:{par}:{cur}"], rg=(3,))
                    ev("vector", bk, lambda e, p=par, c=cur, b_=b:
                       e.tensor_add(Rb[:, p, 1 - c, :], b_[:, 384:512],
                                    Rb[:, p, c, :]),
                       reads=[f"Rb:{par}:{cur}"],
                       writes=[f"Rb:{par}:{1 - cur}"], rg=(3,))
                    cur = 1 - cur
                mm(bk, b[:, 384:512], Ls[:, par, :], Rb[:, par, cur, :],
                   start=True, stop=True,
                   reads=[f"Ls:{par}", f"Rb:{par}:{cur}"], rg=(3,))
                ev("vector", bk, lambda e, p=par, c=cur, hh=h, ss=sc, b_=b:
                   e.tensor_sub(MTs[:, hh, ss, :], Rb[:, p, c, :],
                                b_[:, 384:512]),
                   reads=[f"Rb:{par}:{cur}"], writes=[f"MTs:{h}:{sc}"],
                   rg=(3,))

        S.tag = "back-ema"
        for h in range(H2):
            bk = "S0" if h == 0 else "S1"
            mm(bk, BK[bk][:, 0:MAC], ones1[:],
               gam2[0:1, bf, h * MAC:(h + 1) * MAC], start=True,
               stop=True, reads=["ones1", f"gam2:{bf}"], rg=(0, 1))
            ev("vector", bk, lambda e, b=BK[bk]: e.tensor_copy(geb[:],
                                                               b[:, 0:MAC]),
               writes=["geb"], rg=(0, 1))
            for blk in range(2):
                j = 8 + h * 2 + blk
                vo(lambda e, jj=j: e.tensor_mul(ueb[:], geb[:],
                                                pcv[:, bf, jj, :]),
                   reads=["geb", f"pcv:{bf}:{j}"], writes=["ueb"])
                vo(lambda e, jj=j: e.tensor_sub(ueb[:], pcv[:, bf, jj, :],
                                                ueb[:]),
                   reads=["ueb", f"pcv:{bf}:{j}"], writes=["ueb"])
                vo(lambda e, bb=blk, hh=h: e.tensor_tensor_scan(
                    esc[:], geb[:], ueb[:],
                    ecar[:, hh * 2 + bb:hh * 2 + bb + 1], ALU.mult, ALU.add),
                   reads=["geb", "ueb", "ecar"], writes=["esc"])
                vo(lambda e, bb=blk, hh=h: e.tensor_copy(
                    ecar[:, hh * 2 + bb:hh * 2 + bb + 1],
                    esc[:, MAC - 1:MAC]),
                   reads=["esc"], writes=["ecar"])
                bk2, r0 = btr(2)
                for sc in range(NSUB):
                    tr(bk2, BK[bk2][:, (r0 + sc) * 128:(r0 + sc + 1) * 128],
                       esc[:, sc * 128:(sc + 1) * 128], reads=["esc"],
                       rg=(r0 + sc,))
                ev("vector", bk2, lambda e, hh=h, bb=blk, b=BK[bk2], r=r0:
                   e.tensor_copy(
                       emaTM[:, hh, :, bb * 128:bb * 128 + 128],
                       b[:, r * 128:(r + NSUB) * 128].rearrange(
                           "p (s t) -> p s t", s=NSUB)),
                   writes=[f"emaTM:{h}:{sc}" for sc in range(NSUB)],
                   rg=(r0, r0 + 1))

        if DBG_LEVEL < 8:
            return
        S.tag = "back-B"
        for sc in range(NSUB):
            for h in range(H2):
                bkB = "Ba" if h == 0 else "Bb"
                bkG = "Ca" if h == 0 else "Cb"
                bB, bG = BK[bkB], BK[bkG]
                mm(bkB, bB[:, 0:256], MTs[:, h, sc, :], VbTM[:, h, sc, :],
                   start=True, stop=True,
                   reads=[f"MTs:{h}:{sc}", f"VbTM:{h}:{sc}"])
                ev("vector", bkB, lambda e, hh=h, ss=sc, b_=bB: e.tensor_copy(
                    byu[:, hh, ss, :], b_[:, 0:256]),
                   writes=[f"byu:{h}:{sc}"])
                for blk in range(2):
                    mm(bkG, bG[:, 256 + blk * 128:384 + blk * 128],
                       KbTM[:, h, sc, blk * 128:(blk + 1) * 128],
                       MTs[:, h, sc, :], start=True, stop=True,
                       reads=[f"KbTM:{h}:{sc}", f"MTs:{h}:{sc}"])
                ev("vector", bkG, lambda e, hh=h, ss=sc, b_=bG:
                   e.tensor_scalar(
                       WpTn[:, hh, ss, :, :].rearrange("p b t -> p (b t)"),
                       b_[:, 256:512], -1.0, None, ALU.mult),
                   writes=[f"WpTn:{h}:{sc}"])
                for blk in range(2):
                    mm(bkG, bG[:, 0:128], WpTn[:, h, sc, blk, :],
                       khT[:, h, blk, sc * 128:(sc + 1) * 128],
                       start=(blk == 0), stop=(blk == 1),
                       reads=[f"WpTn:{h}:{sc}", f"khT:{h}:{sc}"])
                for blk in range(2):
                    mm(bkG, bG[:, 128:256],
                       khT[:, h, blk, sc * 128:(sc + 1) * 128],
                       WpTn[:, h, sc, blk, :],
                       start=(blk == 0), stop=(blk == 1),
                       reads=[f"WpTn:{h}:{sc}", f"khT:{h}:{sc}"])
                ev("vector", bkG, lambda e, hh=h, ss=sc, b_=bG: e.tensor_mul(
                    Gns[:, hh, ss, :], b_[:, 0:128], mGL[:]),
                   reads=["mGL"], writes=[f"Gns:{h}:{sc}"])
                ev("vector", bkG, lambda e, hh=h, ss=sc, b_=bG: e.tensor_mul(
                    GTn[:, hh, ss, :], b_[:, 128:256], mGU[:]),
                   reads=["mGU"], writes=[f"GTn:{h}:{sc}"])
                mm(bkG, bG[:, 256:384], Gns[:, h, sc, :], GTn[:, h, sc, :],
                   start=True, stop=True,
                   reads=[f"Gns:{h}:{sc}", f"GTn:{h}:{sc}"])
                ev("vector", bkG, lambda e, hh=h, ss=sc, b_=bG: e.tensor_copy(
                    GT2s[:, hh, ss, :], b_[:, 256:384]),
                   writes=[f"GT2s:{h}:{sc}"])
        for sc in range(NSUB):
            for h in range(H2):
                bkB = "Ba" if h == 0 else "Bb"
                bkG = "Ca" if h == 0 else "Cb"
                pS = pS0 if h == 0 else pS1
                bkS = "S0" if h == 0 else "S1"
                bkO, bO = bkS, pS
                bB, bG = BK[bkB], BK[bkG]
                kS0 = f"S0sb:{h}"
                for blk in range(2):
                    mm(bkB, bB[:, 0:256], WpTn[:, h, sc, blk, :],
                       S0sb[:, h, blk, :], start=(blk == 0), stop=(blk == 1),
                       reads=[f"WpTn:{h}:{sc}", kS0])
                ev("vector", bkB, lambda e, hh=h, ss=sc, b_=bB: e.tensor_add(
                    byu[:, hh, ss, :], byu[:, hh, ss, :], b_[:, 0:256]),
                   reads=[f"byu:{h}:{sc}"], writes=[f"byu:{h}:{sc}"])
                mm(bkB, bB[:, 256:512], GT2s[:, h, sc, :], byu[:, h, sc, :],
                   start=True, stop=True,
                   reads=[f"GT2s:{h}:{sc}", f"byu:{h}:{sc}"])
                ev("vector", bkB, lambda e, hh=h, ss=sc, b_=bB: e.tensor_add(
                    byu[:, hh, ss, :], byu[:, hh, ss, :], b_[:, 256:512]),
                   reads=[f"byu:{h}:{sc}"], writes=[f"byu:{h}:{sc}"])
                mm(bkB, bB[:, 0:256], GTn[:, h, sc, :], byu[:, h, sc, :],
                   start=True, stop=True,
                   reads=[f"GTn:{h}:{sc}", f"byu:{h}:{sc}"])
                ev("vector", bkB, lambda e, hh=h, ss=sc, b_=bB: e.tensor_add(
                    byu[:, hh, ss, :], byu[:, hh, ss, :], b_[:, 0:256]),
                   reads=[f"byu:{h}:{sc}"], writes=[f"byu:{h}:{sc}"])
                for blk in range(2):
                    j = h * 2 + blk
                    mm(bkO, bO[:, 0:256],
                       pcv[:, bf, j, sc * 128:(sc + 1) * 128],
                       S0sb[:, h, blk, :],
                       start=(blk == 0), stop=False,
                       reads=[f"pcv:{bf}:{j}", kS0], rg=(0, 1))
                for blk in range(2):
                    j = h * 2 + blk
                    mm(bkG, bG[:, 384:512],
                       khT[:, h, blk, sc * 128:(sc + 1) * 128],
                       pcv[:, bf, j, sc * 128:(sc + 1) * 128],
                       start=(blk == 0), stop=(blk == 1),
                       reads=[f"khT:{h}:{sc}", f"pcv:{bf}:{j}"], rg=(3,))
                ev("vector", bkG, lambda e, hh=h, b_=bG: e.tensor_mul(
                    ATm[:, hh, :], b_[:, 384:512], mAT[:]),
                   reads=["mAT"], writes=[f"ATm:{h}"], rg=(3,))
                mm(bkO, bO[:, 0:256], ATm[:, h, :], byu[:, h, sc, :],
                   start=False, stop=True,
                   reads=[f"ATm:{h}", f"byu:{h}:{sc}"])
                ev("vector", bkO, lambda e, hh=h, ss=sc, b_=bO: e.tensor_scalar(
                    occ[:, hh, :], b_[:, 0:256], colsP[:, hh, ss, 0:1],
                    None, ALU.mult),
                   reads=[f"colsP:{h}:{sc}"], writes=[f"occ:{h}"])
                for blk in range(2):
                    mm(bkS, pS[:, blk * 256:(blk + 1) * 256],
                       kTM[:, h, sc, blk * 128:(blk + 1) * 128],
                       byu[:, h, sc, :],
                       start=True, stop=True,
                       reads=[f"kTM:{h}:{sc}", f"byu:{h}:{sc}"])
                ev("vector", bkS, lambda e, hh=h, p=pS: e.tensor_add(
                    S0sb[:, hh, :, :].rearrange("p b d -> p (b d)"),
                    S0sb[:, hh, :, :].rearrange("p b d -> p (b d)"), p[:]),
                   reads=[kS0], writes=[kS0])
                vo(lambda e, hh=h, ss=sc: e.scalar_tensor_tensor(
                    occ[:, hh, :], emaTM[:, hh, ss, :],
                    colsP[:, hh, ss, 1:2], occ[:, hh, :], ALU.mult, ALU.add),
                   reads=[f"emaTM:{h}:{sc}", f"colsP:{h}:{sc}", f"occ:{h}"],
                   writes=[f"occ:{h}"])
                co(lambda e, hh=h: e.activation(
                    scr[:, 0:256], occ[:, hh, :], AFT.Square,
                    accum_out=colw[:, 8 + hh * 2:9 + hh * 2]),
                   reads=[f"occ:{h}"], writes=["scr", f"colw8:{h}"])
                co(lambda e, hh=h: e.activation(
                    colw[:, 9 + hh * 2:10 + hh * 2],
                    colw[:, 8 + hh * 2:9 + hh * 2], AFT.Sqrt,
                    scale=1.0 / DK, bias=epsc[:, 0:1]),
                   reads=[f"colw8:{h}", "epsc"], writes=[f"colw9:{h}"])
                vo(lambda e, hh=h: e.reciprocal(
                    colw[:, 9 + hh * 2:10 + hh * 2],
                    colw[:, 9 + hh * 2:10 + hh * 2]),
                   reads=[f"colw9:{h}"], writes=[f"colw9:{h}"])
                slot = (sc * 2 + h) % 2
                vo(lambda e, hh=h: e.tensor_scalar(
                    occ[:, hh, :], occ[:, hh, :],
                    colw[:, 9 + hh * 2:10 + hh * 2], None, ALU.mult),
                   reads=[f"occ:{h}", f"colw9:{h}"], writes=[f"occ:{h}"])
                vo(lambda e, hh=h, sl=slot: e.tensor_mul(
                    oft[:, sl, :], occ[:, hh, :], rmswt[:]),
                   reads=[f"occ:{h}", "rmswt"], writes=[f"oft:{slot}"])
                dma("gpsimd",
                    d_oall.rearrange("s t d -> (s t) d")[
                        t0 + sc * 128:t0 + sc * 128 + 128,
                        h * 256:(h + 1) * 256],
                    oft[:, slot, :], reads=[f"oft:{slot}"])

    NM = (DBG_NMAC or NMAC) if DBG_LEVEL >= 1 else 0
    prevB = None
    for m in range(NM):
        S.begin()
        front(m)
        F = S.take()
        if prevB is None:
            S.emit(F)
        else:
            S.merge(prevB, F)
        S.begin()
        back(m)
        prevB = S.take()
    if prevB is not None:
        S.emit(prevB)

    if not DBG_PHASE2:
        if DBG_DUMP:
            dma("gpsimd", d_out.rearrange("a b -> (a b)"),
                d_oall.rearrange("s t d -> (s t d)"),
                extra=[("dG" + str(i), S.c("dG" + str(i))) for i in range(4)])
        return _finish(nc, S)
    tr("Ca", pCa[0:1, 0:128], rmx[:, 0:1], reads=["rmx"])
    ev("vector", "Ca", lambda e: e.tensor_copy(scr[0:1, 0:128],
                                               pCa[0:1, 0:128]),
       writes=["scr"])
    vo(lambda e: e.memset(scal[0:1, 0:16], 0.0), writes=["scal"])
    vo(lambda e: e.reduce_max(scal[0:1, 0:1], scr[0:1, 0:128], axis=AXL.X),
       reads=["scr"], writes=["scal"])
    dma("gpsimd", d_mxi[:, :], scal[0:1, 0:16], reads=["scal"],
        writes=["d_mxi"])
    S.op("gpsimd", lambda e: e.collective_compute(
        "AllReduce", ALU.max, replica_groups=[list(range(8))],
        ins=[d_mxi.opt()], outs=[d_mxo.opt()]),
        reads=["d_mxi"], writes=["d_mxo"])
    dma("gpsimd", scal[0:1, 0:16], d_mxo[:, :], reads=["d_mxo"],
        writes=["scal"])
    dma("sync", mrw[:], d_msk[:, :], writes=["mrw"],
        extra=[("T", S.c("T")), ("V", S.c("V")), ("C", S.c("C"))])
    mm("Cb", pCb[:, 128:144], ones1[:], mrw[:], start=True, stop=True,
       reads=["mrw", "ones1"])
    ev("vector", "Cb", lambda e: e.tensor_copy(mcl[:], pCb[:, 128:144]),
       writes=["mcl"])
    OALL_DG = [("dG" + str(i), S.c("dG" + str(i))) for i in range(4)]
    for t in range(HALF // 128):
        sl = t % 2
        dma("gpsimd", town[:, sl, :], d_oall[0, t * 128:(t + 1) * 128, :],
            writes=[f"town:{sl}"], extra=OALL_DG)
        dma("gpsimd", tpar[:], d_oall[1, t * 128:(t + 1) * 128, :],
            writes=["tpar"], extra=OALL_DG)
        vo(lambda e, s=sl: e.tensor_scalar(town[:, s, :], town[:, s, :],
                                           mcl[:, 3:4], None, ALU.mult),
           reads=[f"town:{sl}", "mcl"], writes=[f"town:{sl}"])
        vo(lambda e, s=sl: e.scalar_tensor_tensor(
            town[:, s, :], tpar[:], mcl[:, 2:3], town[:, s, :],
            ALU.mult, ALU.add),
           reads=["tpar", "mcl", f"town:{sl}"], writes=[f"town:{sl}"])
        dma("gpsimd", d_agi[t * 128:(t + 1) * 128, :], town[:, sl, :],
            reads=[f"town:{sl}"], writes=["d_agi"])
    S.op("gpsimd", lambda e: e.collective_compute(
        "AllGather", ALU.bypass, replica_groups=[list(range(8))],
        ins=[d_agi.opt()], outs=[d_ago.rearrange("s t d -> (s t) d").opt()]),
        reads=["d_agi"], writes=["d_ago"],
        extra=[("dG" + str(i), S.c("dG" + str(i))) for i in range(4)])

    vo(lambda e: e.reciprocal(scal[0:1, 1:2], scal[0:1, 0:1]),
       reads=["scal"], writes=["scal1"])
    vo(lambda e: e.tensor_mul(scal[0:1, 2:3], scal[0:1, 1:2], scal[0:1, 0:1]),
       reads=["scal", "scal1"], writes=["scal2"])
    vo(lambda e: e.tensor_scalar(scal[0:1, 2:3], scal[0:1, 2:3], -1.0, 2.0,
                                 ALU.mult, ALU.add),
       reads=["scal2"], writes=["scal2"])
    vo(lambda e: e.tensor_mul(scal[0:1, 1:2], scal[0:1, 1:2], scal[0:1, 2:3]),
       reads=["scal1", "scal2"], writes=["scal1"])
    vo(lambda e: e.tensor_scalar(scal[0:1, 3:4], scal[0:1, 1:2], 127.0, None,
                                 ALU.mult), reads=["scal1"], writes=["scal3"])

    dma("sync", fbuf[:].rearrange("p a b -> p (a b)"),
        d_fw.rearrange("p a b -> p (a b)"),
        writes=["fbuf"], extra=[("T", S.c("T")), ("V", S.c("V")),
                                ("C", S.c("C"))])
    dma("sync", wob[:].rearrange("p a b -> p (a b)"),
        d_wo.rearrange("p a b -> p (a b)"), writes=["wob"])
    dma("sync", fbc[:], d_fb, writes=["fbc"])
    for kb in range(8):
        co(lambda e, k=kb: e.activation(scr2[:], fbuf[:, k, :], AFT.Abs,
                                        accum_out=colw[:, 12:13]),
           reads=["fbuf"], writes=["scr2", "colw12"])
        if kb == 0:
            vo(lambda e: e.tensor_copy(colw[:, 13:14], colw[:, 12:13]),
               reads=["colw12"], writes=["colw13"])
        else:
            vo(lambda e: e.tensor_add(colw[:, 13:14], colw[:, 13:14],
                                      colw[:, 12:13]),
               reads=["colw12", "colw13"], writes=["colw13"])
    tr("Cb", pCb[0:1, 0:128], colw[:, 13:14], reads=["colw13"])
    ev("vector", "Cb", lambda e: e.tensor_copy(scr2[0:1, 0:128],
                                               pCb[0:1, 0:128]),
       writes=["scr2"])
    vo(lambda e: e.reduce_sum(scal[0:1, 4:5], scr2[0:1, 0:128], axis=AXL.X),
       reads=["scr2"], writes=["scal4"])
    vo(lambda e: e.tensor_scalar(scal[0:1, 5:6], scal[0:1, 4:5],
                                 1.0 / (D * D), None, ALU.mult),
       reads=["scal4"], writes=["scal5"])
    vo(lambda e: e.tensor_scalar(scal[0:1, 6:7], scal[0:1, 5:6], 0.5, None,
                                 ALU.mult), reads=["scal5"], writes=["scal6"])
    vo(lambda e: e.tensor_mul(scal[0:1, 7:8], scal[0:1, 5:6], scal[0:1, 0:1]),
       reads=["scal5", "scal"], writes=["scal7"])
    vo(lambda e: e.tensor_scalar(scal[0:1, 7:8], scal[0:1, 7:8], REC127,
                                 None, ALU.mult),
       reads=["scal7"], writes=["scal7"])
    for i, skey, scol in [(0, "scal3", 3), (1, "scal7", 7), (2, "scal6", 6)]:
        mm("Ca", pCa[:, i * 128:i * 128 + 1], ones1[:],
           scal[0:1, scol:scol + 1], start=True, stop=True,
           reads=["ones1", skey])
    for i in range(3):
        ev("vector", "Ca", lambda e, ii=i: e.tensor_copy(
            bcc[:, ii:ii + 1], pCa[:, ii * 128:ii * 128 + 1]),
           writes=[f"bcc{i}"])
    for kb in range(8):
        co(lambda e, k=kb: e.activation(scr2[:], fbuf[:, k, :], AFT.Abs),
           reads=["fbuf"], writes=["scr2"])
        co(lambda e, k=kb: e.activation(qib[:], fbuf[:, k, :], AFT.Sign),
           reads=["fbuf"], writes=["qib"])
        vo(lambda e: e.tensor_scalar(scr2[:], scr2[:], bcc[:, 2:3], None,
                                     ALU.is_gt),
           reads=["scr2", "bcc2"], writes=["scr2"])
        vo(lambda e, k=kb: e.tensor_mul(qwb[:, k, :], qib[:], scr2[:]),
           reads=["qib", "scr2"], writes=["qwb"])

    if DBG_P2 == "qw":
        for kb in range(8):
            vo(lambda e, k=kb: e.tensor_copy(scr2[:], qwb[:, k, :]),
               reads=["qwb"], writes=["scr2"])
            dma("gpsimd", d_out[kb * 128:(kb + 1) * 128, 0:1024], scr2[:],
                reads=["scr2"])
    P2F = _it.cycle(["Oa", "Ob"])
    P2B = _it.cycle(["Ba", "Bb"])

    def front2(om):
        u0 = om * MAC2
        pb = om % 2
        qiTb, xTbb = (qiT, xTb) if pb == 0 else (qiT2, xTb2)
        exw = [] if pb == 0 else ["fbuf"]
        S.tag = "p2-front"
        for sub in range(NSUB2):
            gi = om * NSUB + sub
            slx = xt2[:, gi % 2, :]
            kx = f"xt2:{gi % 2}"
            dma("sync", slx, d_xh[u0 + sub * 128:u0 + sub * 128 + 128, :],
                writes=[kx])
            vo(lambda e, s=slx: e.reduce_sum(cols[:, 0:1], s, axis=AXL.X),
               reads=[kx], writes=["cols:0"])
            co(lambda e: e.activation(cols[:, 1:2], cols[:, 0:1], AFT.Copy,
                                      scale=-1.0 / D),
               reads=["cols:0"], writes=["cols:1"])
            co(lambda e, s=slx: e.activation(scr2[:], s, AFT.Square,
                                             bias=cols[:, 1:2],
                                             accum_out=cols[:, 2:3]),
               reads=[kx, "cols:1"], writes=["scr2", "cols:2"])
            co(lambda e: e.activation(cols[:, 3:4], cols[:, 2:3], AFT.Sqrt,
                                      scale=1.0 / D, bias=epsc[:, 0:1]),
               reads=["cols:2", "epsc"], writes=["cols:3"])
            vo(lambda e: e.reciprocal(cols[:, 3:4], cols[:, 3:4]),
               reads=["cols:3"], writes=["cols:3"])
            vo(lambda e, s=slx: e.tensor_scalar(
                xn2[:], s, cols[:, 1:2], cols[:, 3:4], ALU.add, ALU.mult),
               reads=[kx, "cols:1", "cols:3"], writes=["xn2"])
            vo(lambda e: e.tensor_scalar(z0b[:], xn2[:], bcc[:, 0:1], 128.5,
                                         ALU.mult, ALU.add),
               reads=["xn2", "bcc0"], writes=["z0b"])
            QK = ["qib0", "qib1"]
            first = True
            for bit in [128.0, 64.0, 32.0, 16.0, 8.0, 4.0, 2.0, 1.0]:
                src, ksrc = (z0b, "z0b") if first else (zwb, "zwb")
                vo(lambda e, b_=bit, s=src: e.tensor_scalar(
                    qib[:], s[:], b_, b_, ALU.is_ge, ALU.mult),
                   reads=[ksrc], writes=QK)
                vo(lambda e, s=src: e.tensor_sub(zwb[:], s[:], qib[:]),
                   reads=[ksrc] + QK, writes=["zwb"])
                first = False
            vo(lambda e: e.tensor_sub(qib[:], z0b[:], zwb[:]),
               reads=["z0b", "zwb"], writes=QK)
            vo(lambda e: e.tensor_scalar(qib[:], qib[:], 128.0, 127.0,
                                         ALU.subtract, ALU.min),
               reads=QK, writes=QK)
            if DBG_P2 == "qi":
                gi2 = om * NSUB2 + sub
                dma("gpsimd", d_out[(gi2 % 8) * 128:(gi2 % 8) * 128 + 128,
                                    (gi2 // 8) * 1024:(gi2 // 8) * 1024 + 1024],
                    qib[:], reads=["qib0", "qib1"])
            for half in range(2):
                bk = next(P2F)
                for j in range(4):
                    db = half * 4 + j
                    tr(bk, BK[bk][:, j * 128:(j + 1) * 128],
                       qib[:, db * 128:(db + 1) * 128],
                       reads=[f"qib{half}"])
                dst = qiTb[:, half * 4:half * 4 + 4,
                           sub * 128:(sub + 1) * 128]
                ev("scalar", bk,
                   lambda e, d=dst, b=BK[bk]: CP(
                       e, d, b[:].rearrange("p (j t) -> p j t", j=4)),
                   writes=[f"qiT:{pb}:{half * 4 + j}" for j in range(4)]
                   + exw)
            for half in range(2):
                bk = next(P2F)
                for j in range(4):
                    db = half * 4 + j
                    tr(bk, BK[bk][:, j * 128:(j + 1) * 128],
                       slx[:, db * 128:(db + 1) * 128], reads=[kx])
                dst = xTbb[:, half * 4:half * 4 + 4,
                           sub * 128:(sub + 1) * 128]
                ev("scalar", bk,
                   lambda e, d=dst, b=BK[bk]: CP(
                       e, d, b[:].rearrange("p (j t) -> p j t", j=4)),
                   writes=[f"xTb:{pb}:{half * 4 + j}" for j in range(4)]
                   + exw)

    def back2(om):
        u0 = om * MAC2
        pb = om % 2
        qiTb, xTbb = (qiT, xTb) if pb == 0 else (qiT2, xTb2)
        S.tag = "p2-ffn"
        for ob_ in range(8):
            bk = "Ca" if ob_ % 2 == 0 else "Cb"
            for kb in range(8):
                mm(bk, BK[bk][:], qwb[:, kb, ob_ * 128:(ob_ + 1) * 128],
                   qiTb[:, kb, :], start=(kb == 0), stop=(kb == 7),
                   reads=["qwb", f"qiT:{pb}:{kb}"])
            ev("scalar", bk, lambda e, o=ob_, b=BK[bk]: e.activation(
                gbuf[:, o, :], b[:], AFT.Silu, scale=bcc[:, 1:2],
                bias=fbc[:, o:o + 1]),
               reads=["bcc1", "fbc"], writes=[f"gbuf:{ob_}"])
            if DBG_P2 == "g":
                dma("gpsimd", d_out[ob_ * 128:(ob_ + 1) * 128, u0:u0 + MAC2],
                    gbuf[:, ob_, :], reads=[f"gbuf:{ob_}"])

        S.tag = "p2-ox"
        for sub in range(NSUB2):
            r0 = u0 + sub * 128
            for sl in range(2):
                dma("sync", town[:, sl, :], d_oall[sl, r0:r0 + 128, :],
                    writes=[f"town:{sl}"])
            dma("sync", tcand[:, 0, :], d_ago[0, r0:r0 + 128, :],
                reads=["d_ago"], writes=["tcand:0"])
            vo(lambda e: e.tensor_scalar(town[:, 0, :], town[:, 0, :],
                                         mcl[:, 2:3], None, ALU.mult),
               reads=["town:0", "mcl"], writes=["town:0"])
            vo(lambda e: e.scalar_tensor_tensor(
                town[:, 0, :], town[:, 1, :], mcl[:, 3:4], town[:, 0, :],
                ALU.mult, ALU.add),
               reads=["town:1", "mcl", "town:0"], writes=["town:0"])
            vo(lambda e: e.tensor_scalar(tpar[:], tcand[:, 0, :],
                                         mcl[:, 4:5], None, ALU.mult),
               reads=["tcand:0", "mcl"], writes=["tpar"])
            for i in range(1, 8):
                sl = i % 2
                dma("sync", tcand[:, sl, :], d_ago[i, r0:r0 + 128, :],
                    reads=["d_ago"], writes=[f"tcand:{sl}"])
                vo(lambda e, ii=i, s=sl: e.scalar_tensor_tensor(
                    tpar[:], tcand[:, s, :], mcl[:, 4 + ii:5 + ii], tpar[:],
                    ALU.mult, ALU.add),
                   reads=[f"tcand:{sl}", "mcl", "tpar"], writes=["tpar"])
            vo(lambda e: e.tensor_scalar(otm[:, 0, :], town[:, 0, :],
                                         mcl[:, 2:3], None, ALU.mult),
               reads=["town:0", "mcl"], writes=["otm:0"])
            vo(lambda e: e.scalar_tensor_tensor(
                otm[:, 0, :], tpar[:], mcl[:, 3:4], otm[:, 0, :],
                ALU.mult, ALU.add),
               reads=["tpar", "mcl", "otm:0"], writes=["otm:0"])
            vo(lambda e: e.tensor_scalar(otm[:, 1, :], town[:, 0, :],
                                         mcl[:, 3:4], None, ALU.mult),
               reads=["town:0", "mcl"], writes=["otm:1"])
            vo(lambda e: e.scalar_tensor_tensor(
                otm[:, 1, :], tpar[:], mcl[:, 2:3], otm[:, 1, :],
                ALU.mult, ALU.add),
               reads=["tpar", "mcl", "otm:1"], writes=["otm:1"])
            for half in range(2):
                bk = next(P2B)
                for j in range(4):
                    tr(bk, BK[bk][:, j * 128:(j + 1) * 128],
                       otm[:, half, j * 128:(j + 1) * 128],
                       reads=[f"otm:{half}"])
                dst = oT[:, half * 4:half * 4 + 4,
                          sub * 128:(sub + 1) * 128]
                ev("vector", bk,
                   lambda e, d=dst, b=BK[bk]: CP(
                       e, d, b[:].rearrange("p (j t) -> p j t", j=4)),
                   writes=[f"oT:{half * 4 + j}" for j in range(4)])
        if DBG_P2 == "oT":
            for db in range(8):
                dma("gpsimd", d_out[db * 128:(db + 1) * 128, u0:u0 + MAC2],
                    oT[:, db, :], reads=[f"oT:{db}"])
        S.tag = "p2-wo"
        for ob_ in range(8):
            bk = "Ba" if ob_ % 2 == 0 else "Bb"
            for kb in range(8):
                mm(bk, BK[bk][:], wob[:, kb, ob_ * 128:(ob_ + 1) * 128],
                   oT[:, kb, :], start=(kb == 0), stop=(kb == 7),
                   reads=["wob"] + [f"oT:{kb}"])
            par = ob_ % 2
            ev("vector", bk, lambda e, o=ob_, b=BK[bk], p=par:
               e.tensor_mul(outb[:, p, :], b[:], gbuf[:, o, :]),
               reads=[f"gbuf:{ob_}"], writes=[f"outb:{par}"])
            vo(lambda e, o=ob_, p=par, xb=xTbb: e.tensor_add(
                outb[:, p, :], outb[:, p, :], xb[:, o, :]),
               reads=[f"outb:{par}", f"xTb:{pb}:{ob_}"],
               writes=[f"outb:{par}"])
            if DBG_P2 is None:
                dma("gpsimd", d_out[ob_ * 128:(ob_ + 1) * 128, u0:u0 + MAC2],
                    outb[:, par, :], reads=[f"outb:{par}"])

    prevB2 = None
    for om in range(NOM):
        S.begin()
        front2(om)
        F2 = S.take()
        if prevB2 is None:
            S.emit(F2)
        else:
            S.merge(prevB2, F2)
        S.begin()
        back2(om)
        prevB2 = S.take()
    if prevB2 is not None:
        S.emit(prevB2)

    global _SCHED
    _SCHED = S
    return _finish(nc, S)


def _finish(nc, S):
    sems = {}
    for name in (["T", "V", "C", "G"]
                 + [q + str(i) for q in ("dX", "dG") for i in range(4)]):
        sems[name] = nc.alloc_semaphore(name)

    with nc.Block() as block:
        @block.sync
        def _(eng):
            S.play(eng, "sync", sems)

        @block.tensor
        def _(eng):
            S.play(eng, "tensor", sems)

        @block.vector
        def _(eng):
            S.play(eng, "vector", sems)

        @block.scalar
        def _(eng):
            S.play(eng, "scalar", sems)

        @block.gpsimd
        def _(eng):
            S.play(eng, "gpsimd", sems)
            for i in range(4):
                eng.wait_ge(sems["dG" + str(i)], S.c("dG" + str(i)))

    return nc


_NC = None


def _get_nc():
    global _NC
    if _NC is None:
        _NC = build_program()
    return _NC


def kernel(L0_attn_hidden_states, L0_attn_Wq, L0_attn_Wk, L0_attn_Wv,
           L0_attn_conv_q, L0_attn_conv_k, L0_attn_conv_v, L0_attn_Wb,
           L0_attn_Wdec, L0_attn_Wmix, L0_attn_conv_mix, L0_attn_mix_bias,
           L0_attn_rms_w, L0_attn_Wo, L0_ffn_weight, L0_ffn_bias):
    x = np.ascontiguousarray(np.asarray(L0_attn_hidden_states, np.float32))
    Wq = np.asarray(L0_attn_Wq, np.float32)
    Wk = np.asarray(L0_attn_Wk, np.float32)
    Wv = np.asarray(L0_attn_Wv, np.float32)
    cq = np.asarray(L0_attn_conv_q, np.float32)
    ck = np.asarray(L0_attn_conv_k, np.float32)
    cv = np.asarray(L0_attn_conv_v, np.float32)
    Wb = np.asarray(L0_attn_Wb, np.float32)
    Wd = np.asarray(L0_attn_Wdec, np.float32)
    Wm = np.asarray(L0_attn_Wmix, np.float32)
    cmix = np.asarray(L0_attn_conv_mix, np.float32)
    mbias = np.asarray(L0_attn_mix_bias, np.float32)
    rmsw = np.asarray(L0_attn_rms_w, np.float32)
    Wo = np.asarray(L0_attn_Wo, np.float32)
    ffw = np.asarray(L0_ffn_weight, np.float32)
    ffb = np.asarray(L0_ffn_bias, np.float32)

    def blk3(a, inner):
        return np.ascontiguousarray(a.reshape(8, 128, inner).transpose(1, 0, 2))

    r = np.arange(128)
    bi, bj = r[:, None] // CH, r[None, :] // CH
    mLTh = ((bi == bj) & (r[None, :] > r[:, None])).astype(np.float32)
    mLh = ((bi == bj) & (r[None, :] < r[:, None])).astype(np.float32)
    mGLh = (bi > bj).astype(np.float32)
    mGUh = (bj > bi).astype(np.float32)
    mATh = ((bj > bi) | ((bi == bj) & (r[None, :] >= r[:, None]))
            ).astype(np.float32)

    def _mk_mask(c):
        hp = c % 2
        m = np.zeros((1, 16), np.float32)
        m[0, 2] = 1.0 if hp == 0 else 0.0
        m[0, 3] = 1.0 - m[0, 2]
        m[0, 4 + (c ^ 1)] = 1.0
        return m

    in_maps = []
    for c in range(8):
        b, hp = c // 2, c % 2
        hs = slice(hp * 512, (hp + 1) * 512)
        wgm = np.stack([Wb[:, 2 * hp], Wb[:, 2 * hp + 1],
                        Wd[:, 2 * hp], Wd[:, 2 * hp + 1],
                        Wm[:, 2 * hp], Wm[:, 2 * hp + 1]], axis=1)
        cvm = np.concatenate([cq[hs], ck[hs], cv[hs]], 0)
        in_maps.append({
            "x": np.ascontiguousarray(x[b]),
            "xh": np.ascontiguousarray(x[b, hp * HALF:(hp + 1) * HALF]),
            "wq": blk3(np.ascontiguousarray(Wq[:, hs]), 512),
            "wk": blk3(np.ascontiguousarray(Wk[:, hs]), 512),
            "wv": blk3(np.ascontiguousarray(Wv[:, hs]), 512),
            "wg": blk3(np.ascontiguousarray(wgm), 6),
            "cvw": np.ascontiguousarray(
                cvm.reshape(12, 128, 4).transpose(1, 0, 2)),
            "cmx": np.ascontiguousarray(cmix[2 * hp:2 * hp + 2]),
            "mb2": np.ascontiguousarray(mbias[2 * hp:2 * hp + 2, None]),
            "rw": np.ascontiguousarray(np.tile(rmsw[None, :], (128, 1))),
            "idm": np.eye(128, dtype=np.float32),
            "mLT": mLTh, "mL": mLh, "mGL": mGLh, "mGU": mGUh, "mAT": mATh,
            "msk": _mk_mask(c),
            "wo": blk3(np.ascontiguousarray(Wo), 1024),
            "fw": blk3(np.ascontiguousarray(ffw.T), 1024),
            "fb": np.ascontiguousarray(ffb.reshape(8, 128).T),
        })

    nc = _get_nc()
    res = run_bass_kernel_spmd(nc, in_maps, core_ids=list(range(8)))
    out = np.empty((B, L, D), np.float32)
    for c in range(8):
        b, hp = c // 2, c % 2
        out[b, hp * HALF:(hp + 1) * HALF, :] = res.results[c]["out"].T
    return out
